# version 02
# baseline (speedup 1.0000x reference)
import numpy as np
import ml_dtypes
import contextlib
import concourse.bass as bass
import concourse.mybir as mybir
from concourse.bass_utils import run_bass_kernel_spmd

F32 = mybir.dt.float32
F16 = mybir.dt.float16
I32 = mybir.dt.int32
AF = mybir.ActivationFunctionType
ALU = mybir.AluOpType
AXX = mybir.AxisListType.X

H = 1024
S = 8192
SC = 1024
NCH = S // SC
NK = 8
NL = 4
NN = 16
LN2 = float(np.log(2.0))
GROUPS = [[0, 1], [2, 3], [4, 5], [6, 7]]

PC_CW0, PC_CW1, PC_CB0, PC_CB1, PC_DTW, PC_DTB, PC_D, PC_MG, PC_MB = 0, 4, 8, 9, 10, 11, 12, 13, 14

BLK = 128 * 1024
XBLK = 128 * NK * 33
OFF_INT = 0
OFF_OUTT = OFF_INT + NL * 16 * BLK
OFF_XT = OFF_OUTT + NL * 8 * BLK
OFF_OP1 = OFF_XT + NL * XBLK
OFF_OP2 = OFF_OP1 + 8 * BLK
OFF_FFN = OFF_OP2 + 8 * BLK
WSC_SZ = OFF_FFN + 8 * BLK


class Sem:
    def __init__(self, stk, nc, name):
        self.h = stk.enter_context(nc.semaphore(name))
        self.v = 0

    def reserve(self, k):
        self.v += k
        return self.v


class KB:
    ENGS = ("sync", "vector", "scalar", "tensor", "gpsimd")

    def __init__(self, nc):
        self.nc = nc
        self.script = []
        self.waited = {e: {} for e in self.ENGS}

    def em(self, eng, build, waits=(), inc=None, inck=1):
        real = []
        wd = self.waited[eng]
        for s, v in waits:
            if s is None or v <= 0:
                continue
            if wd.get(id(s), -1) >= v:
                continue
            wd[id(s)] = v
            real.append((s.h, v))
        if inc is not None:
            inc.reserve(inck)
        tgt = inc.v if inc is not None else None

        def fn(e, rw=real, build=build, inc=inc, inck=inck):
            for h, v in rw:
                e.wait_ge(h, v)
            ins = build(e)
            if inc is not None:
                ins.then_inc(inc.h, inck)

        self.script.append((eng, fn))
        return tgt

    def play(self, block):
        for eng_name in self.ENGS:
            deco = getattr(block, eng_name)
            ents = [fn for e, fn in self.script if e == eng_name]

            def body(e, ents=ents):
                for fn in ents:
                    fn(e)

            deco(body)


def build_program(n_layers=NL, do_epi=True, dbg_on=False):
    nc = bass.Bass("TRN2", target_bir_lowering=False)

    x0t = nc.dram_tensor("x0t", [H, S], F32, kind="ExternalInput").ap()
    winT = nc.dram_tensor("winT", [NL, H, 2048], F32, kind="ExternalInput").ap()
    woutT = nc.dram_tensor("woutT", [NL, H, H], F32, kind="ExternalInput").ap()
    wxT = nc.dram_tensor("wxT", [NL, H, 33], F32, kind="ExternalInput").ap()
    psml = nc.dram_tensor("psml", [NL * H, 16], F32, kind="ExternalInput").ap()
    phh = nc.dram_tensor("phh", [H, 12], F32, kind="ExternalInput").ap()
    pop = nc.dram_tensor("pop", [H, 4], F32, kind="ExternalInput").ap()
    wop1T = nc.dram_tensor("wop1T", [H, H], F32, kind="ExternalInput").ap()
    wop2T = nc.dram_tensor("wop2T", [H, H], F32, kind="ExternalInput").ap()
    wffn = nc.dram_tensor("wffn", [H, H], F32, kind="ExternalInput").ap()
    identf = nc.dram_tensor("identf", [128, 128], F32, kind="ExternalInput").ap()
    self16 = nc.dram_tensor("self16", [33, 33 * 128], F16, kind="ExternalInput").ap()
    out = nc.dram_tensor("out", [H, S], F32, kind="ExternalOutput").ap()
    dbg = None
    dbg2 = None
    if dbg_on:
        dbg = nc.dram_tensor("dbg", [n_layers + 2, H, SC], F32, kind="ExternalOutput").ap()
        dbg2 = nc.dram_tensor("dbg2", [8, H, SC], F32, kind="ExternalOutput").ap()

    wsc = nc.dram_tensor("wsc", [WSC_SZ], F16).ap()
    cc_prj_i = nc.dram_tensor("cc_prj_i", [33, SC], F32).ap()
    cc_prj_o = nc.dram_tensor("cc_prj_o", [33, SC], F32).ap()
    cc_st_i = nc.dram_tensor("cc_st_i", [1, 2 * SC], F32).ap()
    cc_st_o = nc.dram_tensor("cc_st_o", [1, 2 * SC], F32).ap()
    cc_dl_i = nc.dram_tensor("cc_dl_i", [H, SC], F16).ap()
    cc_dl_o = nc.dram_tensor("cc_dl_o", [H, SC], F16).ap()

    with contextlib.ExitStack() as stk:
        sbt = lambda n, sh, d: stk.enter_context(nc.sbuf_tensor(n, sh, d))
        pst_ = lambda n, sh, d: stk.enter_context(nc.psum_tensor(n, sh, d))

        x_sb = sbt("x_sb", [128, NK * SC], F32)
        xiab = sbt("xiab", [128, 2 * NK * (SC + 3)], F16)
        sres = sbt("sres", [128, NK * SC], F16)
        xn_sb = sbt("xn_sb", [128, NK * SC], F16)
        ar1 = sbt("ar1", [128, NK * SC], F32)
        warena = sbt("warena", [128, 3 * 1024], F16)
        a_t = sbt("a_t", [128, SC], F32)
        a_t2 = sbt("a_t2", [128, SC], F32)
        u_t = sbt("u_t", [128, SC], F16)
        h_t = sbt("h_t", [128, SC], F16)
        Bbc = sbt("Bbc", [128, 2 * SC], F16)
        Cbc = sbt("Cbc", [128, 2 * SC], F16)
        rstd = sbt("rstd", [128, SC], F32)
        negm = sbt("negm", [128, SC], F32)
        scr1 = sbt("scr1", [128, SC], F32)
        scr2 = sbt("scr2", [128, SC], F32)
        scr3 = sbt("scr3", [128, SC], F32)
        scr4 = sbt("scr4", [128, SC], F32)
        psml_sb = sbt("psml_sb", [128, NL * NK * 16], F32)
        phh_sb = sbt("phh_sb", [128, NK * 12], F32)
        pop_sb = sbt("pop_sb", [128, NK * 4], F32)
        ident_sb = sbt("ident_sb", [128, 128], F32)
        ones32 = sbt("ones32", [128, 128], F32)
        ones16 = sbt("ones16", [128, 128], F16)
        sel_sb = sbt("sel_sb", [33, 33 * 128], F16)
        proj_sb = sbt("proj_sb", [33, SC], F32)
        projb = sbt("projb", [33, SC], F16)
        st_sb = sbt("st_sb", [128, NL * NK * NN], F16)
        halo = sbt("halo", [128, NL * NK * 2 * 3], F16)
        srow = sbt("srow", [1, 2 * SC], F32)
        onerow = sbt("onerow", [1, 128], F32)
        wq1 = sbt("wq1", [128, 32], F32)
        wq2 = sbt("wq2", [128, 32], F32)
        wq3 = sbt("wq3", [128, 32], F32)
        lncol = sbt("lncol", [128, 8], F32)
        cst = sbt("cst", [128, 24], F32)

        pm0 = pst_("pm0", [128, 512], F32)
        pm1 = pst_("pm1", [128, 512], F32)
        pb0 = pst_("pb0", [128, 1024], F32)
        pb1 = pst_("pb1", [128, 1024], F32)
        pstt = pst_("pstt", [128, 1024], F32)

        sw = Sem(stk, nc, "sw")
        sg = Sem(stk, nc, "sg")
        sp = Sem(stk, nc, "sp")
        sa = Sem(stk, nc, "sa")
        sv = Sem(stk, nc, "sv")
        sout = Sem(stk, nc, "sout")

        block = stk.enter_context(nc.Block())
        kb = KB(nc)

        def dmaS(dst, src, waits=()):
            return kb.em("sync", lambda e, d=dst, s=src: e.dma_start(out=d, in_=s),
                         waits=waits, inc=sw, inck=16)

        def dmaG(dst, src, waits=()):
            return kb.em("gpsimd", lambda e, d=dst, s=src: e.dma_start(out=d, in_=s),
                         waits=waits, inc=sg, inck=16)

        def xia(k):
            return xiab[:, k * (SC + 3) : (k + 1) * (SC + 3)]

        def xib(k):
            o = NK * (SC + 3)
            return xiab[:, o + k * (SC + 3) : o + (k + 1) * (SC + 3)]

        dt_v = ar1[:, 0 : NK * SC // 2].bitcast(F16)
        y_v = ar1[:, NK * SC // 2 : NK * SC].bitcast(F16)
        dtk = lambda k: dt_v[:, k * SC : (k + 1) * SC]
        yk = lambda k: y_v[:, k * SC : (k + 1) * SC]
        pcol = lambda l, k, c: psml_sb[:, (l * NK + k) * 16 + c : (l * NK + k) * 16 + c + 1]
        hcol = lambda k, c: phh_sb[:, k * 12 + c : k * 12 + c + 1]
        opcol = lambda k, c: pop_sb[:, k * 4 + c : k * 4 + c + 1]
        stcol = lambda l, k, n: st_sb[:, (l * NK + k) * NN + n : (l * NK + k) * NN + n + 1]
        halo3 = lambda l, k, cv: halo[:, ((l * NK + k) * 2 + cv) * 3 : ((l * NK + k) * 2 + cv) * 3 + 3]
        xt = [x_sb[:, k * SC : (k + 1) * SC] for k in range(NK)]

        G = {"pst": (None, 0), "pb0": (None, 0), "pb1": (None, 0),
             "pm0": (None, 0), "pm1": (None, 0)}

        dmaS(psml_sb[:, :].rearrange("p (l k c) -> p l k c", l=NL, k=NK), psml.rearrange("(l k p) c -> p l k c", p=128, k=NK))
        dmaS(phh_sb[:, :].rearrange("p (k c) -> p k c", k=NK), phh.rearrange("(k p) c -> p k c", p=128))
        dmaS(pop_sb[:, :].rearrange("p (k c) -> p k c", k=NK), pop.rearrange("(k p) c -> p k c", p=128))
        m_ident = dmaS(ident_sb[:, :], identf[:, :])
        m_sel = dmaS(sel_sb[:, :], self16[:, :])

        def _initv(e):
            e.memset(cst[:, 0:1], 3 * LN2)
            e.memset(cst[:, 1:2], -LN2)
            e.memset(cst[:, 2:3], -3 * LN2)
            e.memset(cst[:, 3:4], LN2)
            e.memset(cst[:, 4:5], -0.5)
            e.memset(cst[:, 5:6], 1e-5)
            for _n in range(16):
                e.memset(cst[:, 8 + _n : 9 + _n], -float(_n + 1))
            e.memset(ones32[:, :], 1.0)
            e.memset(ones16[:, :], 1.0)
            e.memset(onerow[:, :], 1.0)
            e.memset(st_sb[:, :], 0.0)
            e.memset(halo[:, :], 0.0)
            return e.memset(lncol[:, :], 0.0)

        m_initv = kb.em("vector", _initv, inc=sv)

        stage32 = x_sb[:, 0:1024]
        stage16v = xn_sb[:, 0:1024]
        last_cast = m_initv
        last_store = 0

        def repack(src2d, off, ncols):
            nonlocal last_cast, last_store
            n = NK * ncols
            m_in = dmaS(stage32[:, 0:n].rearrange("p (k c) -> p k c", k=NK),
                        src2d.rearrange("(k p) c -> p k c", p=128),
                        waits=[(sv, last_cast)])
            last_cast = kb.em("vector",
                              lambda e, n=n: e.tensor_copy(stage16v[:, 0:n], stage32[:, 0:n]),
                              waits=[(sw, max(m_in, last_store))], inc=sv)
            last_store = dmaS(wsc[off : off + 128 * n].rearrange("(p c) -> p c", p=128),
                              stage16v[:, 0:n], waits=[(sv, last_cast)])
            return last_store

        for l in range(NL):
            for m in range(16):
                repack(winT[l, :, m * 128 : (m + 1) * 128], OFF_INT + (l * 16 + m) * BLK, 128)
            for m in range(8):
                repack(woutT[l, :, m * 128 : (m + 1) * 128], OFF_OUTT + (l * 8 + m) * BLK, 128)
            repack(wxT[l, :, :], OFF_XT + l * XBLK, 33)
        for m in range(8):
            repack(wop1T[:, m * 128 : (m + 1) * 128], OFF_OP1 + m * BLK, 128)
            repack(wop2T[:, m * 128 : (m + 1) * 128], OFF_OP2 + m * BLK, 128)
        m_repack = last_store

        def quant_dequant(src, dst, waits0):
            r32 = lambda t: t.rearrange("p (w c) -> p w c", c=32)
            m_rd = kb.em("vector", lambda e: e.tensor_reduce(
                wq1[:, :].rearrange("p (w c) -> p w c", c=1), r32(src), AXX, ALU.max,
                apply_absolute_value=True), waits=waits0, inc=sv)
            m_cl = kb.em("gpsimd", lambda e: e.tensor_scalar(wq2[:, :], wq1[:, :], 1.0 / 7.5, 1e-30, ALU.mult, ALU.max),
                         waits=[(sv, m_rd)], inc=sg, inck=1)
            m_inv = kb.em("vector", lambda e: e.reciprocal(wq3[:, :], wq2[:, :]),
                          waits=[(sg, m_cl)], inc=sv)
            m_rl = kb.em("gpsimd", lambda e: e.tensor_copy(wq1[:, :], wq3[:, :]),
                         waits=[(sv, m_inv)], inc=sg, inck=1)
            bc3 = wq1[:, :].rearrange("p w -> p w ()").broadcast_to([128, 32, 32])
            kb.em("vector", lambda e: e.tensor_tensor(r32(scr3[:, :]), r32(src), bc3, ALU.mult),
                  waits=[(sg, m_rl)])
            kb.em("vector", lambda e: e.tensor_tensor(scr4[:, :], scr3[:, :], scr3[:, :], ALU.mult))
            kb.em("vector", lambda e: e.tensor_scalar(dst, scr4[:, :], 4.0, None, ALU.is_ge))
            kb.em("vector", lambda e: e.tensor_scalar(h_t[:, :], scr4[:, :], 16.0, None, ALU.is_ge))
            m_s = kb.em("vector", lambda e: e.tensor_tensor(dst, dst, h_t[:, :], ALU.add), inc=sv)
            m_si = kb.em("scalar", lambda e: e.activation(scr4[:, :], dst, AF.Exp, bias=cst[:, 0:1], scale=cst[:, 1:2]),
                         waits=[(sv, m_s)], inc=sa)
            m_st = kb.em("scalar", lambda e: e.activation(dst, dst, AF.Exp, bias=cst[:, 2:3], scale=cst[:, 3:4]),
                         inc=sa)
            m_w = kb.em("vector", lambda e: e.tensor_tensor(scr3[:, :], scr3[:, :], scr4[:, :], ALU.mult),
                        waits=[(sa, m_si)], inc=sv)
            kb.em("vector", lambda e: e.tensor_copy(scr4[:, :].bitcast(I32)[:, 0:SC], scr3[:, :]))
            m_r = kb.em("vector", lambda e: e.tensor_copy(scr3[:, :], scr4[:, :].bitcast(I32)[:, 0:SC]), inc=sv)
            kb.em("vector", lambda e: e.tensor_tensor(dst, scr3[:, :], dst, ALU.mult),
                  waits=[(sa, m_st), (sv, m_r)])
            bc2 = wq2[:, :].rearrange("p w -> p w ()").broadcast_to([128, 32, 32])
            m_q = kb.em("vector", lambda e: e.tensor_tensor(r32(dst), r32(dst), bc2, ALU.mult), inc=sv)
            return m_q

        for i in range(8):
            m_in = dmaS(scr2[:, :], wffn[i * 128 : (i + 1) * 128, :], waits=[(sv, last_cast)])
            mq = quant_dequant(scr2[:, :], scr1[:, :], [(sw, m_in), (sv, m_initv)])
            m_ev = 0
            for j in range(8):
                pmt = pm0 if j % 2 == 0 else pm1
                gk = "pm0" if j % 2 == 0 else "pm1"
                m_tr = kb.em("tensor", lambda e, j=j, pmt=pmt: e.transpose(
                    pmt[:, 0:128], scr1[:, j * 128 : (j + 1) * 128], ident_sb[:, :]),
                    waits=[(sv, mq), (sw, m_ident), G[gk]], inc=sp)
                m_ev = kb.em("scalar", lambda e, j=j, pmt=pmt: e.copy(
                    stage16v[:, j * 128 : (j + 1) * 128], pmt[:, 0:128]),
                    waits=[(sp, m_tr), (sw, m_repack)], inc=sa)
                G[gk] = (sa, m_ev)
            dmaS(wsc[OFF_FFN + i * BLK : OFF_FFN + (i + 1) * BLK].rearrange("(p c) -> p c", p=128),
                 stage16v[:, 0:1024], waits=[(sa, m_ev)])
        m_setup = sw.v

        ring_free = [(None, 0)] * 3
        ring_state = [0]

        def ring_load(off, ncols=1024):
            sl = ring_state[0] % 3
            ring_state[0] += 1
            g = ring_free[sl]
            slot = warena[:, sl * 1024 : sl * 1024 + ncols]
            m = dmaS(slot, wsc[off : off + 128 * ncols].rearrange("(p c) -> p c", p=128),
                     waits=[g, (sw, m_setup)])
            return sl, slot, m

        def ln_stats2(tiles, rstd_t, negm_t, dep, src_f16=False):
            rstd_t = rstd_t[:, :]
            negm_t = negm_t[:, :]
            ones_t = ones16 if src_f16 else ones32
            m_mm = 0
            for nh in range(2):
                for k in range(NK):
                    first = (k == 0 and nh == 0)
                    m_mm = kb.em("tensor", lambda e, k=k, nh=nh, ot=ones_t: e.matmul(
                        pstt[:, nh * 512 : (nh + 1) * 512], ot[:, :],
                        tiles[k][:, nh * 512 : (nh + 1) * 512],
                        start=(k == 0), stop=(k == NK - 1)),
                        waits=([dep, G["pst"]] if first else [dep]),
                        inc=(sp if (k == NK - 1 and nh == 1) else None))
            m_mu = kb.em("vector", lambda e, nt=negm_t: e.tensor_scalar(nt, pstt[:, :], 1.0 / 2048.0 if src_f16 else 1.0 / H, None, ALU.mult),
                         waits=[(sp, m_mm)], inc=sv)
            sq_guard = [0, 0, 0, 0]
            m_mm2 = 0
            for k in range(NK):
                d = (scr1, scr2, scr3, scr4)[k % 4]
                dv = d[:, :].bitcast(F16)[:, 0:SC] if src_f16 else d[:, :]
                w = [(sp, sq_guard[k % 4])] if sq_guard[k % 4] else [dep]
                m_sq = kb.em("scalar", lambda e, k=k, dv=dv: e.activation(dv, tiles[k][:, :], AF.Square),
                             waits=w, inc=sa)
                for nh in range(2):
                    m_mm2 = kb.em("tensor", lambda e, k=k, nh=nh, dv=dv, ot=ones_t: e.matmul(
                        pstt[:, nh * 512 : (nh + 1) * 512], ot[:, :],
                        dv[:, nh * 512 : (nh + 1) * 512],
                        start=(k == 0), stop=(k == NK - 1)),
                        waits=[(sa, m_sq)] + ([(sv, m_mu)] if (k == 0 and nh == 0) else []),
                        inc=sp)
                sq_guard[k % 4] = m_mm2
            m_v1 = kb.em("vector", lambda e: e.tensor_scalar(scr3[:, :], pstt[:, :], 1.0 / 2048.0 if src_f16 else 1.0 / H, None, ALU.mult),
                         waits=[(sp, m_mm2)], inc=sv)
            kb.em("vector", lambda e, nt=negm_t: e.tensor_tensor(scr4[:, :], nt, nt, ALU.mult))
            m_v2 = kb.em("vector", lambda e: e.tensor_tensor(scr3[:, :], scr3[:, :], scr4[:, :], ALU.subtract), inc=sv)
            G["pst"] = (sv, m_v1)
            m_l = kb.em("scalar", lambda e: e.activation(scr3[:, :], scr3[:, :], AF.Ln, bias=cst[:, 5:6]),
                        waits=[(sv, m_v2)], inc=sa)
            kb.em("scalar", lambda e, rt=rstd_t: e.activation(rt, scr3[:, :], AF.Exp, scale=cst[:, 4:5]), inc=sa)
            m_n = kb.em("vector", lambda e, nt=negm_t, rt=rstd_t: e.scalar_tensor_tensor(
                nt, nt, -1.0, rt, ALU.mult, ALU.mult), waits=[(sa, sa.v)], inc=sv)
            return m_n

        last_out = 0

        for c in range(NCH):
            m_x0 = dmaS(x_sb[:, :].rearrange("p (k t) -> p k t", k=NK),
                        x0t[:, c * SC : (c + 1) * SC].rearrange("(k p) t -> p k t", p=128),
                        waits=[(sv, sv.v), (sa, sa.v), (sp, sp.v), (sg, sg.v), (sout, last_out)])
            tt = [ar1[:, k * SC : (k + 1) * SC] for k in range(NK)]
            m_n = ln_stats2(xt, rstd, negm, (sw, m_x0))
            m_t = 0
            for k in range(NK):
                kb.em("vector", lambda e, k=k: e.tensor_tensor(scr1[:, :], xt[k], rstd[:, :], ALU.mult),
                      waits=[(sv, m_n)])
                m_t = kb.em("vector", lambda e, k=k: e.tensor_tensor(tt[k], scr1[:, :], negm[:, :], ALU.add),
                            inc=sv)
            m_n2 = ln_stats2(tt, rstd, negm, (sv, m_t))
            x_dep = None
            m_xr = 0
            for k in range(NK):
                kb.em("vector", lambda e, k=k: e.tensor_tensor(scr1[:, :], tt[k], rstd[:, :], ALU.mult),
                      waits=[(sv, m_n2)])
                m_s2 = kb.em("vector", lambda e: e.tensor_tensor(scr2[:, :], scr1[:, :], negm[:, :], ALU.add),
                             waits=[(sa, m_xr)], inc=sv)
                m_xr = kb.em("scalar", lambda e, k=k: e.activation(
                    xt[k], scr2[:, :], AF.Identity, bias=hcol(k, 9), scale=hcol(k, 8)),
                    waits=[(sv, m_s2)], inc=sa)
            x_dep = (sa, m_xr)

            if dbg is not None and c == 0:
                for k in range(NK):
                    dmaS(dbg[0, k * 128 : (k + 1) * 128, :], xt[k], waits=[x_dep])

            for l in range(n_layers):
                m_n3 = ln_stats2(xt, rstd, negm, x_dep)
                m_xn = 0
                for k in range(NK):
                    kb.em("vector", lambda e, k=k: e.tensor_tensor(scr1[:, :], xt[k], rstd[:, :], ALU.mult),
                          waits=[(sv, m_n3)])
                    m_s2 = kb.em("vector", lambda e: e.tensor_tensor(scr2[:, :], scr1[:, :], negm[:, :], ALU.add),
                                 waits=[(sa, m_xn)], inc=sv)
                    m_xn = kb.em("scalar", lambda e, k=k, l=l: e.activation(
                        xn_sb[:, k * SC : (k + 1) * SC], scr2[:, :], AF.Identity,
                        bias=hcol(k, 4 + l), scale=hcol(k, l)),
                        waits=[(sv, m_s2)], inc=sa)

                if dbg2 is not None and c == 0 and l == -1:
                    for k in range(NK):
                        kb.em("gpsimd", lambda e, k=k: e.dma_start(
                            out=dbg2[0, k * 128 : (k + 1) * 128, :], in_=xn_sb[:, k * SC : (k + 1) * SC]),
                            waits=[(sa, m_xn)], inc=sg, inck=16)
                for k in range(NK):
                    kb.em("vector", lambda e, k=k, l=l: e.tensor_copy(xia(k)[:, 0:3], halo3(l, k, 0)))
                    kb.em("vector", lambda e, k=k, l=l: e.tensor_copy(xib(k)[:, 0:3], halo3(l, k, 1)),
                          inc=(sv if k == NK - 1 else None))
                m_halo = sv.v

                for m in range(16):
                    sl, slot, m_w = ring_load(OFF_INT + (l * 16 + m) * BLK)
                    m_mm_last = 0
                    for nh in range(2):
                        bi = nh
                        pmt = pm0 if bi == 0 else pm1
                        gk = "pm0" if bi == 0 else "pm1"
                        m_mm = 0
                        for k in range(NK):
                            m_mm = kb.em("tensor", lambda e, k=k, nh=nh, pmt=pmt, slot=slot: e.matmul(
                                pmt[:, :], slot[:, k * 128 : (k + 1) * 128],
                                xn_sb[:, k * SC + nh * 512 : k * SC + (nh + 1) * 512],
                                start=(k == 0), stop=(k == NK - 1)),
                                waits=[(sa, m_xn), (sw, m_w)] + ([G[gk]] if k == 0 else []),
                                inc=(sp if k == NK - 1 else None))
                        if m < 8:
                            dst = xia(m)[:, 3 + nh * 512 : 3 + (nh + 1) * 512]
                            fn = AF.Silu
                        else:
                            dst = sres[:, (m - 8) * SC + nh * 512 : (m - 8) * SC + (nh + 1) * 512]
                            fn = AF.Sigmoid
                        m_ev = kb.em("scalar", lambda e, dst=dst, fn=fn, pmt=pmt: e.activation(dst, pmt[:, :], fn),
                                     waits=[(sp, m_mm), (sv, m_halo)], inc=sa)
                        G[gk] = (sa, m_ev)
                        m_mm_last = m_mm
                    ring_free[sl] = (sp, m_mm_last)
                m_evin = sa.v

                m_xi = 0
                for k in range(NK):
                    s3f = scr3[:, :].bitcast(F16)[:, 0:SC]
                    s4f = scr4[:, :].bitcast(F16)[:, 0:SC]
                    kb.em("vector", lambda e, k=k, l=l, s3f=s3f: e.tensor_scalar(
                        s3f, xia(k)[:, 3 : 3 + SC], pcol(l, k, PC_CW0 + 3), None, ALU.mult),
                        waits=[(sa, m_evin), (sa, m_xi)])
                    for j in range(3):
                        kb.em("vector", lambda e, k=k, l=l, j=j, s4f=s4f: e.tensor_scalar(
                            s4f, xia(k)[:, j : j + SC], pcol(l, k, PC_CW0 + j), None, ALU.mult))
                        kb.em("vector", lambda e, s3f=s3f, s4f=s4f: e.tensor_tensor(s3f, s3f, s4f, ALU.add))
                    m_c1 = kb.em("vector", lambda e, k=k, l=l: e.tensor_copy(halo3(l, k, 0), xia(k)[:, SC : SC + 3]),
                                 inc=sv)
                    m_c1a = kb.em("scalar", lambda e, k=k, l=l, s3f=s3f: e.activation(
                        xib(k)[:, 3 : 3 + SC], s3f, AF.Silu, bias=pcol(l, k, PC_CB0)),
                        waits=[(sv, m_c1)], inc=sa)
                    kb.em("vector", lambda e, k=k, l=l, s3f=s3f: e.tensor_scalar(
                        s3f, xib(k)[:, 3 : 3 + SC], pcol(l, k, PC_CW1 + 3), None, ALU.mult),
                        waits=[(sa, m_c1a)])
                    for j in range(3):
                        kb.em("vector", lambda e, k=k, l=l, j=j, s4f=s4f: e.tensor_scalar(
                            s4f, xib(k)[:, j : j + SC], pcol(l, k, PC_CW1 + j), None, ALU.mult))
                        kb.em("vector", lambda e, s3f=s3f, s4f=s4f: e.tensor_tensor(s3f, s3f, s4f, ALU.add))
                    m_c2 = kb.em("vector", lambda e, k=k, l=l: e.tensor_copy(halo3(l, k, 1), xib(k)[:, SC : SC + 3]),
                                 inc=sv)
                    m_xi = kb.em("scalar", lambda e, k=k, l=l, s3f=s3f: e.activation(
                        xia(k)[:, 3 : 3 + SC], s3f, AF.Silu, bias=pcol(l, k, PC_CB1)),
                        waits=[(sv, m_c2)], inc=sa)

                if dbg2 is not None and c == 0 and l == -1:
                    for k in range(NK):
                        kb.em("gpsimd", lambda e, k=k: e.dma_start(
                            out=dbg2[1, k * 128 : (k + 1) * 128, :], in_=xia(k)[:, 3 : 3 + SC]),
                            waits=[(sa, m_xi)], inc=sg, inck=16)
                        kb.em("gpsimd", lambda e, k=k: e.dma_start(
                            out=dbg2[6, k * 128 : (k + 1) * 128, :], in_=xib(k)[:, 3 : 3 + SC]),
                            waits=[(sa, m_xi)], inc=sg, inck=16)
                        kb.em("gpsimd", lambda e, k=k: e.dma_start(
                            out=dbg2[2, k * 128 : (k + 1) * 128, :], in_=sres[:, k * SC : (k + 1) * SC]),
                            waits=[(sa, m_xi)], inc=sg, inck=16)
                m_px = 0
                sl, slotx, m_wx = ring_load(OFF_XT + l * XBLK, NK * 33)
                m_mm_x = 0
                for nh in range(2):
                    pmt = pm0 if nh == 0 else pm1
                    gk = "pm0" if nh == 0 else "pm1"
                    m_mm = 0
                    for k in range(NK):
                        m_mm = kb.em("tensor", lambda e, k=k, nh=nh, pmt=pmt, slotx=slotx: e.matmul(
                            pmt[0:33, :], slotx[:, k * 33 : (k + 1) * 33],
                            xia(k)[:, 3 + nh * 512 : 3 + (nh + 1) * 512],
                            start=(k == 0), stop=(k == NK - 1)),
                            waits=[(sa, m_xi), (sw, m_wx)] + ([G[gk]] if k == 0 else []),
                            inc=(sp if k == NK - 1 else None))
                    m_px = kb.em("vector", lambda e, nh=nh, pmt=pmt: e.tensor_copy(
                        proj_sb[0:33, nh * 512 : (nh + 1) * 512], pmt[0:33, :]),
                        waits=[(sp, m_mm)], inc=sv)
                    G[gk] = (sv, m_px)
                    m_mm_x = m_mm
                ring_free[sl] = (sp, m_mm_x)
                m_yi = 0
                for k in range(NK):
                    m_yi = kb.em("vector", lambda e, k=k, l=l: e.tensor_scalar(
                        yk(k), xia(k)[:, 3 : 3 + SC], pcol(l, k, PC_D), None, ALU.mult),
                        waits=[(sa, m_xi)], inc=sv)
                m_gi = dmaG(cc_prj_i[:, :], proj_sb[:, :], waits=[(sv, m_px)])
                m_cc1 = kb.em("gpsimd", lambda e: e.collective_compute(
                    "AllReduce", ALU.add, ins=[cc_prj_i[:, :].opt()], outs=[cc_prj_o[:, :].opt()],
                    replica_groups=GROUPS), waits=[(sg, m_gi)], inc=sg, inck=1)
                m_gb = dmaG(proj_sb[:, :], cc_prj_o[:, :], waits=[(sg, m_cc1)])
                m_pjb = kb.em("vector", lambda e: e.tensor_copy(projb[:, :], proj_sb[:, :]),
                              waits=[(sg, m_gb)], inc=sv)

                m_dtbc = 0
                for nh in range(2):
                    m_dtbc = kb.em("tensor", lambda e, nh=nh: e.matmul(
                        pb0[:, nh * 512 : (nh + 1) * 512], sel_sb[:, 0:128],
                        projb[0:33, nh * 512 : (nh + 1) * 512], start=True, stop=True),
                        waits=[(sv, m_pjb), (sw, m_sel)] + ([G["pb0"]] if nh == 0 else []),
                        inc=sp)
                m_dt = 0
                m_ab = 0
                for k in range(NK):
                    kb.em("vector", lambda e, k=k, l=l: e.tensor_scalar(
                        scr1[:, :], pb0[:, :], pcol(l, k, PC_DTW), pcol(l, k, PC_DTB), ALU.mult, ALU.add),
                        waits=[(sp, m_dtbc), (sa, m_ab)])
                    m_z = kb.em("vector", lambda e: e.tensor_scalar_mul(scr2[:, :], scr1[:, :], 1.0),
                                inc=sv)
                    m_ab = kb.em("scalar", lambda e: e.activation(scr2[:, :], scr1[:, :], AF.Abs),
                                 waits=[(sv, m_z)], inc=sa)
                    m_e = kb.em("scalar", lambda e: e.activation(scr3[:, :], scr2[:, :], AF.Exp, scale=-1.0),
                                inc=sa)
                    m_l1 = kb.em("scalar", lambda e: e.activation(scr2[:, :], scr3[:, :], AF.Ln, bias=1.0), inc=sa)
                    kb.em("vector", lambda e: e.tensor_scalar(scr4[:, :], scr1[:, :], 0.0, None, ALU.max))
                    m_dt = kb.em("vector", lambda e, k=k: e.tensor_tensor(dtk(k), scr2[:, :], scr4[:, :], ALU.add),
                                 waits=[(sa, m_l1)], inc=sv)
                G["pb0"] = (sv, m_dt)

                if dbg2 is not None and c == 0 and l == -1:
                    kb.em("gpsimd", lambda e: e.dma_start(
                        out=dbg2[7, 0:33, :], in_=proj_sb[:, :]),
                        waits=[(sv, m_pjb)], inc=sg, inck=16)
                    for k in range(NK):
                        kb.em("gpsimd", lambda e, k=k: e.dma_start(
                            out=dbg2[3, k * 128 : (k + 1) * 128, :], in_=dtk(k)),
                            waits=[(sv, m_dt)], inc=sg, inck=16)
                m_dtx = 0
                m_ag_guard = [0, 0]
                for k in range(NK):
                    m_dtx = kb.em("vector", lambda e, k=k: e.tensor_tensor(
                        xib(k)[:, 3 : 3 + SC], dtk(k), xia(k)[:, 3 : 3 + SC], ALU.mult), inc=sv)
                m_scan_hist = [m_dtx, m_dtx]
                m_scan = m_dtx
                m_ev = [[0, 0] for _ in range(NN)]

                def emit_bcast(n):
                    mB = 0
                    for nh in range(2):
                        mB = kb.em("tensor", lambda e, n=n, nh=nh: e.matmul(
                            pb0[:, nh * 512 : (nh + 1) * 512], sel_sb[:, (1 + n) * 128 : (2 + n) * 128],
                            projb[0:33, nh * 512 : (nh + 1) * 512], start=True, stop=True),
                            waits=([G["pb0"]] if nh == 0 else []), inc=sp)
                    mC = 0
                    for nh in range(2):
                        mC = kb.em("tensor", lambda e, n=n, nh=nh: e.matmul(
                            pb1[:, nh * 512 : (nh + 1) * 512], sel_sb[:, (17 + n) * 128 : (18 + n) * 128],
                            projb[0:33, nh * 512 : (nh + 1) * 512], start=True, stop=True),
                            waits=([G["pb1"]] if nh == 0 else []), inc=sp)
                    sl = n % 2
                    m_evB = kb.em("scalar", lambda e, sl=sl: e.copy(Bbc[:, sl * SC : (sl + 1) * SC], pb0[:, :]),
                                  waits=[(sp, mB), (sv, m_scan_hist[0])], inc=sa)
                    m_evC = kb.em("scalar", lambda e, sl=sl: e.copy(Cbc[:, sl * SC : (sl + 1) * SC], pb1[:, :]),
                                  waits=[(sp, mC), (sv, m_scan_hist[0])], inc=sa)
                    G["pb0"] = (sa, m_evB)
                    G["pb1"] = (sa, m_evC)
                    m_ev[n] = [m_evB, m_evC]

                emit_bcast(0)
                for n in range(NN):
                    if n + 1 < NN:
                        emit_bcast(n + 1)
                    sl = n % 2
                    m_evB, m_evC = m_ev[n]
                    for k in range(NK):
                        at = a_t if k % 2 == 0 else a_t2
                        m_ag = kb.em("scalar", lambda e, k=k, n=n, at=at: e.activation(
                            at[:, :], dtk(k), AF.Identity, bias=1.0, scale=cst[:, 8 + n : 9 + n]),
                            waits=[(sv, m_ag_guard[k % 2]), (sv, m_dt)], inc=sa)
                        kb.em("vector", lambda e, k=k, sl=sl: e.tensor_tensor(
                            u_t[:, :], xib(k)[:, 3 : 3 + SC], Bbc[:, sl * SC : (sl + 1) * SC], ALU.mult),
                            waits=[(sa, m_evB), (sa, m_evC), (sv, m_dtx)])
                        m_sc = kb.em("vector", lambda e, k=k, n=n, l=l, at=at: e.tensor_tensor_scan(
                            h_t[:, :], at[:, :], u_t[:, :], stcol(l, k, n), ALU.mult, ALU.add),
                            waits=[(sg, sg.v), (sa, m_ag)], inc=sv)
                        m_ag_guard[k % 2] = m_sc
                        kb.em("gpsimd", lambda e, k=k, n=n, l=l: e.tensor_copy(
                            stcol(l, k, n), h_t[:, SC - 1 : SC]), waits=[(sv, m_sc)], inc=sg, inck=1)
                        kb.em("vector", lambda e, sl=sl: e.tensor_tensor(
                            u_t[:, :], h_t[:, :], Cbc[:, sl * SC : (sl + 1) * SC], ALU.mult))
                        m_scan = kb.em("vector", lambda e, k=k: e.tensor_tensor(
                            yk(k), yk(k), u_t[:, :], ALU.add), inc=sv)
                    m_scan_hist = [m_scan, m_scan_hist[0]]

                m_ys = 0
                for nh in range(2):
                    for k in range(NK):
                        first = (k == 0 and nh == 0)
                        m_ys = kb.em("tensor", lambda e, k=k, nh=nh: e.matmul(
                            pstt[:, nh * 512 : (nh + 1) * 512], ones16[:, :],
                            y_v[:, k * SC + nh * 512 : k * SC + (nh + 1) * 512],
                            start=(k == 0), stop=(k == NK - 1)),
                            waits=[(sv, m_scan)] + ([G["pst"]] if first else []),
                            inc=(sp if (k == NK - 1 and nh == 1) else None))
                m_r1 = kb.em("vector", lambda e: e.tensor_copy(srow[:, 0:SC], pstt[0:1, :]),
                             waits=[(sp, m_ys)], inc=sv)
                sq_guard = [0, 0, 0, 0]
                m_yq = 0
                for k in range(NK):
                    d = (scr1, scr2, scr3, scr4)[k % 4]
                    dv = d[:, :].bitcast(F16)[:, 0:SC]
                    w = [(sp, sq_guard[k % 4])] if sq_guard[k % 4] else [(sv, m_scan)]
                    m_sq = kb.em("scalar", lambda e, k=k, dv=dv: e.activation(dv, yk(k), AF.Square),
                                 waits=w, inc=sa)
                    for nh in range(2):
                        m_yq = kb.em("tensor", lambda e, k=k, nh=nh, dv=dv: e.matmul(
                            pstt[:, nh * 512 : (nh + 1) * 512], ones16[:, :],
                            dv[:, nh * 512 : (nh + 1) * 512],
                            start=(k == 0), stop=(k == NK - 1)),
                            waits=[(sa, m_sq)] + ([(sv, m_r1)] if (k == 0 and nh == 0) else []),
                            inc=sp)
                    sq_guard[k % 4] = m_yq
                m_r2 = kb.em("vector", lambda e: e.tensor_copy(srow[:, SC : 2 * SC], pstt[0:1, :]),
                             waits=[(sp, m_yq)], inc=sv)
                G["pst"] = (sv, m_r2)
                m_gi2 = dmaG(cc_st_i[:, :], srow[:, :], waits=[(sv, m_r2)])
                m_cc2 = kb.em("gpsimd", lambda e: e.collective_compute(
                    "AllReduce", ALU.add, ins=[cc_st_i[:, :].opt()], outs=[cc_st_o[:, :].opt()],
                    replica_groups=GROUPS), waits=[(sg, m_gi2)], inc=sg, inck=1)
                m_gb2 = dmaG(srow[:, :], cc_st_o[:, :], waits=[(sg, m_cc2)])
                m_bc = 0
                for half in range(2):
                    pbt = pb0 if half == 0 else pb1
                    gk = "pb0" if half == 0 else "pb1"
                    for nh in range(2):
                        m_bc = kb.em("tensor", lambda e, half=half, nh=nh, pbt=pbt: e.matmul(
                            pbt[:, nh * 512 : (nh + 1) * 512], onerow[:, :],
                            srow[:, half * SC + nh * 512 : half * SC + (nh + 1) * 512],
                            start=True, stop=True),
                            waits=[(sg, m_gb2)] + ([G[gk]] if nh == 0 else []), inc=sp)
                DIT = 2048.0
                m_mu = kb.em("vector", lambda e: e.tensor_scalar(negm[:, :], pb0[:, :], 1.0 / DIT, None, ALU.mult),
                             waits=[(sp, m_bc)], inc=sv)
                kb.em("vector", lambda e: e.tensor_scalar(scr1[:, :], pb1[:, :], 1.0 / DIT, None, ALU.mult))
                kb.em("vector", lambda e: e.tensor_tensor(scr2[:, :], negm[:, :], negm[:, :], ALU.mult))
                m_v2 = kb.em("vector", lambda e: e.tensor_tensor(scr1[:, :], scr1[:, :], scr2[:, :], ALU.subtract),
                             inc=sv)
                G["pb0"] = (sv, m_v2)
                G["pb1"] = (sv, m_v2)
                m_l2 = kb.em("scalar", lambda e: e.activation(scr1[:, :], scr1[:, :], AF.Ln, bias=cst[:, 5:6]),
                             waits=[(sv, m_v2)], inc=sa)
                m_r3 = kb.em("scalar", lambda e: e.activation(rstd[:, :], scr1[:, :], AF.Exp, scale=cst[:, 4:5]), inc=sa)
                m_nm = kb.em("vector", lambda e: e.scalar_tensor_tensor(
                    negm[:, :], negm[:, :], -1.0, rstd[:, :], ALU.mult, ALU.mult),
                    waits=[(sa, m_r3)], inc=sv)

                m_yh = 0
                m_g = 0
                for k in range(NK):
                    s3f = scr3[:, :].bitcast(F16)[:, 0:SC]
                    kb.em("vector", lambda e, k=k: e.tensor_tensor(scr1[:, :], yk(k), rstd[:, :], ALU.mult),
                          waits=[(sv, m_nm), (sa, m_g)])
                    m_a = kb.em("vector", lambda e: e.tensor_tensor(scr2[:, :], scr1[:, :], negm[:, :], ALU.add),
                                inc=sv)
                    m_g = kb.em("scalar", lambda e, k=k, l=l, s3f=s3f: e.activation(
                        s3f, scr2[:, :], AF.Identity, bias=pcol(l, k, PC_MB), scale=pcol(l, k, PC_MG)),
                        waits=[(sv, m_a)], inc=sa)
                    m_yh = kb.em("vector", lambda e, k=k, s3f=s3f: e.tensor_tensor(
                        xia(k)[:, 3 : 3 + SC], s3f, sres[:, k * SC : (k + 1) * SC], ALU.mult),
                        waits=[(sa, m_g)], inc=sv)

                if dbg2 is not None and c == 0 and l == -1:
                    for k in range(NK):
                        kb.em("gpsimd", lambda e, k=k: e.dma_start(
                            out=dbg2[5, k * 128 : (k + 1) * 128, :], in_=xia(k)[:, 3 : 3 + SC]),
                            waits=[(sv, m_yh)], inc=sg, inck=16)
                m_dl_dma = 0
                for m in range(8):
                    sl, slot, m_w = ring_load(OFF_OUTT + (l * 8 + m) * BLK)
                    m_evs = []
                    m_mm_last = 0
                    for nh in range(2):
                        pmt = pm0 if nh == 0 else pm1
                        gk = "pm0" if nh == 0 else "pm1"
                        m_mm = 0
                        for k in range(NK):
                            m_mm = kb.em("tensor", lambda e, k=k, nh=nh, pmt=pmt, slot=slot: e.matmul(
                                pmt[:, :], slot[:, k * 128 : (k + 1) * 128],
                                xia(k)[:, 3 + nh * 512 : 3 + (nh + 1) * 512],
                                start=(k == 0), stop=(k == NK - 1)),
                                waits=[(sv, m_yh), (sw, m_w)] + ([G[gk]] if k == 0 else []),
                                inc=(sp if k == NK - 1 else None))
                        m_ev = kb.em("vector", lambda e, pmt=pmt, nh=nh: e.tensor_copy(
                            u_t[:, nh * 512 : (nh + 1) * 512], pmt[:, :]),
                            waits=[(sp, m_mm), (sg, m_dl_dma)], inc=sv)
                        G[gk] = (sv, m_ev)
                        m_evs.append(m_ev)
                        m_mm_last = m_mm
                    m_dl_dma = dmaG(cc_dl_i[m * 128 : (m + 1) * 128, :], u_t[:, :],
                                    waits=[(sv, max(m_evs))])
                    ring_free[sl] = (sp, m_mm_last)
                m_cc3 = kb.em("gpsimd", lambda e: e.collective_compute(
                    "AllReduce", ALU.add, ins=[cc_dl_i[:, :].opt()], outs=[cc_dl_o[:, :].opt()],
                    replica_groups=GROUPS), waits=[(sg, m_dl_dma)], inc=sg, inck=1)
                m_xu = 0
                for k in range(NK):
                    m_gk = dmaG(u_t[:, :], cc_dl_o[k * 128 : (k + 1) * 128, :],
                                waits=[(sg, m_cc3), (sv, m_xu)])
                    m_xu = kb.em("vector", lambda e, k=k: e.tensor_tensor(xt[k], xt[k], u_t[:, :], ALU.add),
                                 waits=[(sg, m_gk)], inc=sv)
                x_dep = (sv, m_xu)
                if dbg is not None and c == 0:
                    for k in range(NK):
                        dmaS(dbg[l + 1, k * 128 : (k + 1) * 128, :], xt[k], waits=[x_dep])

            if do_epi:
                m_xfb = 0
                for k in range(NK):
                    m_xfb = kb.em("vector", lambda e, k=k: e.tensor_copy(
                        xn_sb[:, k * SC : (k + 1) * SC], xt[k]), waits=[x_dep], inc=sv)
                z1 = sres
                for m in range(8):
                    sl, slot, m_w = ring_load(OFF_OP1 + m * BLK)
                    m_mm_last = 0
                    for nh in range(2):
                        pmt = pm0 if nh == 0 else pm1
                        gk = "pm0" if nh == 0 else "pm1"
                        m_mm = 0
                        for k in range(NK):
                            m_mm = kb.em("tensor", lambda e, k=k, nh=nh, pmt=pmt, slot=slot: e.matmul(
                                pmt[:, :], slot[:, k * 128 : (k + 1) * 128],
                                xn_sb[:, k * SC + nh * 512 : k * SC + (nh + 1) * 512],
                                start=(k == 0), stop=(k == NK - 1)),
                                waits=[(sv, m_xfb), (sw, m_w)] + ([G[gk]] if k == 0 else []),
                                inc=(sp if k == NK - 1 else None))
                        m_ev = kb.em("scalar", lambda e, m=m, nh=nh, pmt=pmt: e.activation(
                            z1[:, m * SC + nh * 512 : m * SC + (nh + 1) * 512], pmt[:, :],
                            AF.Gelu, bias=opcol(m, 0)),
                            waits=[(sp, m_mm)], inc=sa)
                        G[gk] = (sa, m_ev)
                        m_mm_last = m_mm
                    ring_free[sl] = (sp, m_mm_last)
                m_z1 = sa.v
                m_x0b = dmaS(ar1[:, :].rearrange("p (k t) -> p k t", k=NK),
                             x0t[:, c * SC : (c + 1) * SC].rearrange("(k p) t -> p k t", p=128),
                             waits=[(sv, sv.v), (sa, sa.v)])
                x0l = [ar1[:, k * SC : (k + 1) * SC] for k in range(NK)]
                m_n0 = ln_stats2(x0l, rstd, negm, (sw, m_x0b))
                aov = xiab[:, :].bitcast(F32)
                ao = [aov[:, k * SC : (k + 1) * SC] for k in range(NK)]
                m_ao = 0
                for m in range(8):
                    sl, slot, m_w = ring_load(OFF_OP2 + m * BLK)
                    m_mm_last = 0
                    for nh in range(2):
                        pmt = pm0 if nh == 0 else pm1
                        gk = "pm0" if nh == 0 else "pm1"
                        m_mm = 0
                        for k in range(NK):
                            m_mm = kb.em("tensor", lambda e, k=k, nh=nh, pmt=pmt, slot=slot: e.matmul(
                                pmt[:, :], slot[:, k * 128 : (k + 1) * 128],
                                z1[:, k * SC + nh * 512 : k * SC + (nh + 1) * 512],
                                start=(k == 0), stop=(k == NK - 1)),
                                waits=[(sa, m_z1), (sw, m_w)] + ([G[gk]] if k == 0 else []),
                                inc=(sp if k == NK - 1 else None))
                        kb.em("vector", lambda e, m=m, nh=nh: e.tensor_tensor(
                            scr1[:, nh * 512 : (nh + 1) * 512],
                            x0l[m][:, nh * 512 : (nh + 1) * 512],
                            rstd[:, nh * 512 : (nh + 1) * 512], ALU.mult),
                            waits=[(sv, m_n0), (sp, m_mm)])
                        kb.em("vector", lambda e, nh=nh: e.tensor_tensor(
                            scr1[:, nh * 512 : (nh + 1) * 512],
                            scr1[:, nh * 512 : (nh + 1) * 512],
                            negm[:, nh * 512 : (nh + 1) * 512], ALU.add))
                        m_ev = kb.em("vector", lambda e, m=m, nh=nh, pmt=pmt: e.scalar_tensor_tensor(
                            ao[m][:, nh * 512 : (nh + 1) * 512], pmt[:, :], opcol(m, 1),
                            scr1[:, nh * 512 : (nh + 1) * 512], ALU.add, ALU.add), inc=sv)
                        G[gk] = (sv, m_ev)
                        m_ao = m_ev
                        m_mm_last = m_mm
                    ring_free[sl] = (sp, m_mm_last)
                m_n1 = ln_stats2(ao, rstd, negm, (sv, m_ao))
                m_x1 = 0
                for k in range(NK):
                    kb.em("vector", lambda e, k=k: e.tensor_tensor(scr1[:, :], ao[k], rstd[:, :], ALU.mult),
                          waits=[(sv, m_n1)])
                    kb.em("vector", lambda e: e.tensor_tensor(scr1[:, :], scr1[:, :], negm[:, :], ALU.add))
                    m_x1 = kb.em("vector", lambda e, k=k: e.tensor_tensor(xt[k], x0l[k], scr1[:, :], ALU.add),
                                 inc=sv)
                if dbg is not None and c == 0:
                    for k in range(NK):
                        dmaS(dbg[n_layers + 1, k * 128 : (k + 1) * 128, :], xt[k], waits=[(sv, m_x1)])
                fhat = xiab[:, :].bitcast(F16)
                m_fh = 0
                for t in range(8):
                    for k in range(NK):
                        pmt = pm0 if k % 2 == 0 else pm1
                        gk = "pm0" if k % 2 == 0 else "pm1"
                        m_tr = kb.em("tensor", lambda e, k=k, t=t, pmt=pmt: e.transpose(
                            pmt[:, 0:128], xt[k][:, t * 128 : (t + 1) * 128], ident_sb[:, :]),
                            waits=[(sv, m_x1), G[gk]], inc=sp)
                        m_cp = kb.em("vector", lambda e, k=k, pmt=pmt: e.tensor_copy(
                            scr1[:, k * 128 : (k + 1) * 128], pmt[:, 0:128]),
                            waits=[(sp, m_tr)], inc=sv)
                        G[gk] = (sv, m_cp)
                    m_tt = sv.v
                    m_r0 = kb.em("vector", lambda e: e.tensor_reduce(lncol[:, 0:1], scr1[:, :], AXX, ALU.add),
                          waits=[(sv, m_tt)], inc=sv)
                    m_sq = kb.em("scalar", lambda e: e.activation(scr2[:, :], scr1[:, :], AF.Square),
                                 waits=[(sv, m_tt)], inc=sa)
                    m_r1c = kb.em("vector", lambda e: e.tensor_reduce(lncol[:, 1:2], scr2[:, :], AXX, ALU.add),
                          waits=[(sa, m_sq)], inc=sv)
                    m_cm = kb.em("gpsimd", lambda e: (
                        e.tensor_scalar(lncol[:, 2:3], lncol[:, 0:1], 1.0 / H, None, ALU.mult),
                        e.tensor_scalar(lncol[:, 3:4], lncol[:, 1:2], 1.0 / H, None, ALU.mult),
                        e.tensor_tensor(lncol[:, 4:5], lncol[:, 2:3], lncol[:, 2:3], ALU.mult),
                        e.tensor_tensor(lncol[:, 3:4], lncol[:, 3:4], lncol[:, 4:5], ALU.subtract))[-1],
                        waits=[(sv, m_r1c)], inc=sg, inck=1)
                    m_ln = kb.em("scalar", lambda e: e.activation(lncol[:, 5:6], lncol[:, 3:4], AF.Ln, bias=cst[:, 5:6]),
                                 waits=[(sg, m_cm)], inc=sa)
                    m_rs = kb.em("scalar", lambda e: e.activation(lncol[:, 6:7], lncol[:, 5:6], AF.Exp, scale=cst[:, 4:5]),
                                 inc=sa)
                    m_ng = kb.em("gpsimd", lambda e: (
                        e.tensor_tensor(lncol[:, 7:8], lncol[:, 2:3], lncol[:, 6:7], ALU.mult),
                        e.tensor_scalar(lncol[:, 7:8], lncol[:, 7:8], -1.0, None, ALU.mult))[-1],
                        waits=[(sa, m_rs)], inc=sg, inck=1)
                    m_ft = kb.em("scalar", lambda e: e.activation(
                        scr2[:, :], scr1[:, :], AF.Identity, bias=lncol[:, 7:8], scale=lncol[:, 6:7]),
                        waits=[(sg, m_ng)], inc=sa)
                    mq = quant_dequant(scr2[:, :], scr1[:, :], [(sa, m_ft)])
                    for k in range(NK):
                        pmt = pm0 if k % 2 == 0 else pm1
                        gk = "pm0" if k % 2 == 0 else "pm1"
                        m_tr = kb.em("tensor", lambda e, k=k, pmt=pmt: e.transpose(
                            pmt[:, 0:128], scr1[:, k * 128 : (k + 1) * 128], ident_sb[:, :]),
                            waits=[(sv, mq), G[gk]], inc=sp)
                        m_fh = kb.em("scalar", lambda e, k=k, t=t, pmt=pmt: e.copy(
                            fhat[:, k * SC + t * 128 : k * SC + (t + 1) * 128], pmt[:, 0:128]),
                            waits=[(sp, m_tr)], inc=sa)
                        G[gk] = (sa, m_fh)
                if dbg2 is not None and c == 0:
                    for k in range(NK):
                        kb.em("gpsimd", lambda e, k=k: e.dma_start(
                            out=dbg2[0, k * 128 : (k + 1) * 128, :], in_=fhat[:, k * SC : (k + 1) * SC]),
                            waits=[(sa, m_fh)], inc=sg, inck=16)
                ffo = [ar1[:, k * SC : (k + 1) * SC] for k in range(NK)]
                m_ffo = 0
                for m in range(8):
                    sl, slot, m_w = ring_load(OFF_FFN + m * BLK)
                    m_mm_last = 0
                    for nh in range(2):
                        pmt = pm0 if nh == 0 else pm1
                        gk = "pm0" if nh == 0 else "pm1"
                        m_mm = 0
                        for k in range(NK):
                            m_mm = kb.em("tensor", lambda e, k=k, nh=nh, pmt=pmt, slot=slot: e.matmul(
                                pmt[:, :], slot[:, k * 128 : (k + 1) * 128],
                                fhat[:, k * SC + nh * 512 : k * SC + (nh + 1) * 512],
                                start=(k == 0), stop=(k == NK - 1)),
                                waits=[(sa, m_fh), (sw, m_w), (sv, m_x1)] + ([G[gk]] if k == 0 else []),
                                inc=(sp if k == NK - 1 else None))
                        m_ev = kb.em("scalar", lambda e, m=m, nh=nh, pmt=pmt: e.activation(
                            ffo[m][:, nh * 512 : (nh + 1) * 512], pmt[:, :], AF.Identity,
                            bias=opcol(m, 2)),
                            waits=[(sp, m_mm)], inc=sa)
                        G[gk] = (sa, m_ev)
                        m_ffo = m_ev
                        m_mm_last = m_mm
                    ring_free[sl] = (sp, m_mm_last)
                if dbg2 is not None and c == 0:
                    for k in range(NK):
                        kb.em("gpsimd", lambda e, k=k: e.dma_start(
                            out=dbg2[1, k * 128 : (k + 1) * 128, :], in_=ffo[k]),
                            waits=[(sa, m_ffo)], inc=sg, inck=16)
                m_n2e = ln_stats2(ffo, rstd, negm, (sa, m_ffo))
                out_guard = [0, 0]
                for k in range(NK):
                    dd = scr2 if k % 2 == 0 else scr4
                    kb.em("vector", lambda e, k=k: e.tensor_tensor(scr1[:, :], ffo[k], rstd[:, :], ALU.mult),
                          waits=[(sv, m_n2e), (sout, out_guard[k % 2])])
                    kb.em("vector", lambda e: e.tensor_tensor(scr1[:, :], scr1[:, :], negm[:, :], ALU.add))
                    m_x2 = kb.em("vector", lambda e, k=k, dd=dd: e.tensor_tensor(dd[:, :], scr1[:, :], xt[k], ALU.add),
                                 inc=sv)
                    last_out = kb.em("sync", lambda e, k=k, c=c, dd=dd: e.dma_start(
                        out=out[k * 128 : (k + 1) * 128, c * SC : (c + 1) * SC], in_=dd[:, :]),
                        waits=[(sv, m_x2)], inc=sout, inck=16)
                    out_guard[k % 2] = last_out

        kb.em("sync", lambda e: e.nop(), waits=[(sout, sout.v)])
        kb.play(block)
    return nc


def host_prep(inputs, n_layers=NL):
    hs = np.asarray(inputs["L0_attn_hidden_states"], np.float32)
    in_proj = np.asarray(inputs["L0_attn_in_proj_w"], np.float32)
    conv_w = np.asarray(inputs["L0_attn_conv_w"], np.float32)
    conv_b = np.asarray(inputs["L0_attn_conv_b"], np.float32)
    x_proj = np.asarray(inputs["L0_attn_x_proj_w"], np.float32)
    dt_w = np.asarray(inputs["L0_attn_dt_proj_w"], np.float32)
    dt_b = np.asarray(inputs["L0_attn_dt_proj_b"], np.float32)
    Dp = np.asarray(inputs["L0_attn_D"], np.float32)
    mg = np.asarray(inputs["L0_attn_mamba_ln_g"], np.float32)
    mb = np.asarray(inputs["L0_attn_mamba_ln_b"], np.float32)
    outp = np.asarray(inputs["L0_attn_out_proj_w"], np.float32)
    lng = np.asarray(inputs["L0_attn_ln_g"], np.float32)
    lnb = np.asarray(inputs["L0_attn_ln_b"], np.float32)
    ing = np.asarray(inputs["L0_attn_in_norm_g"], np.float32)
    inb = np.asarray(inputs["L0_attn_in_norm_b"], np.float32)
    op1 = np.asarray(inputs["L0_attn_op1_w"], np.float32)
    op1b = np.asarray(inputs["L0_attn_op1_b"], np.float32)
    op2 = np.asarray(inputs["L0_attn_op2_w"], np.float32)
    op2b = np.asarray(inputs["L0_attn_op2_b"], np.float32)
    ffw = np.asarray(inputs["L0_ffn_weight"], np.float32)
    ffb = np.asarray(inputs["L0_ffn_bias"], np.float32)

    ident = np.eye(128, dtype=np.float32)
    sel = np.zeros((33, 33 * 128), np.float16)
    for n in range(33):
        sel[n, n * 128 : (n + 1) * 128] = 1.0

    phh_a = np.zeros((H, 12), np.float32)
    phh_a[:, 0:4] = lng.T
    phh_a[:, 4:8] = lnb.T
    phh_a[:, 8] = ing
    phh_a[:, 9] = inb
    pop_a = np.zeros((H, 4), np.float32)
    pop_a[:, 0] = op1b
    pop_a[:, 1] = op2b
    pop_a[:, 2] = ffb

    in_maps = []
    for core in range(8):
        b, hf = core // 2, core % 2
        cols = slice(hf * 1024, (hf + 1) * 1024)
        winT_c = np.zeros((NL, H, 2048), np.float32)
        woutT_c = np.zeros((NL, H, H), np.float32)
        wxT_c = np.zeros((NL, H, 33), np.float32)
        psml_c = np.zeros((NL * H, 16), np.float32)
        for l in range(NL):
            wi = np.concatenate([in_proj[l][hf * 1024 : (hf + 1) * 1024],
                                 in_proj[l][2048 + hf * 1024 : 2048 + (hf + 1) * 1024]], axis=0)
            winT_c[l] = np.ascontiguousarray(wi.T)
            woutT_c[l] = np.ascontiguousarray(outp[l][:, cols].T)
            wxT_c[l] = np.ascontiguousarray(x_proj[l][:, cols].T)
            pm = np.zeros((H, 16), np.float32)
            pm[:, 0:4] = conv_w[l, 0][cols, :]
            pm[:, 4:8] = conv_w[l, 1][cols, :]
            pm[:, 8] = conv_b[l, 0][cols]
            pm[:, 9] = conv_b[l, 1][cols]
            pm[:, 10] = dt_w[l, cols, 0]
            pm[:, 11] = dt_b[l, cols]
            pm[:, 12] = Dp[l, cols]
            pm[:, 13] = mg[l, cols]
            pm[:, 14] = mb[l, cols]
            psml_c[l * H : (l + 1) * H] = pm
        in_maps.append({
            "x0t": np.ascontiguousarray(hs[b].T),
            "winT": winT_c,
            "woutT": woutT_c,
            "wxT": wxT_c,
            "psml": psml_c,
            "phh": phh_a,
            "pop": pop_a,
            "wop1T": np.ascontiguousarray(op1.T),
            "wop2T": np.ascontiguousarray(op2.T),
            "wffn": ffw,
            "identf": ident,
            "self16": sel,
        })
    return in_maps


_NC_CACHE = {}


def kernel(**inputs):
    key = "full"
    if key not in _NC_CACHE:
        _NC_CACHE[key] = build_program()
    nc = _NC_CACHE[key]
    in_maps = host_prep(inputs)
    res = run_bass_kernel_spmd(nc, in_maps, core_ids=list(range(8)))
    outs = []
    for b in range(4):
        o = np.asarray(res.results[2 * b]["out"], np.float32)
        outs.append(o.T)
    return np.stack(outs, axis=0)


# version 03 (final)
# speedup vs baseline: 1.2172x; 1.2172x over previous
import numpy as np
import ml_dtypes
import contextlib
import concourse.bass as bass
import concourse.mybir as mybir
from concourse.bass_utils import run_bass_kernel_spmd

F32 = mybir.dt.float32
F16 = mybir.dt.float16
I32 = mybir.dt.int32
AF = mybir.ActivationFunctionType
ALU = mybir.AluOpType
AXX = mybir.AxisListType.X

H = 1024
S = 8192
SC = 1024
NCH = S // SC
NK = 8
NL = 4
NN = 16
LN2 = float(np.log(2.0))
GROUPS = [[0, 1], [2, 3], [4, 5], [6, 7]]

PC_CW0, PC_CW1, PC_CB0, PC_CB1, PC_DTW, PC_DTB, PC_D, PC_MG, PC_MB = 0, 4, 8, 9, 10, 11, 12, 13, 14

BLK = 128 * 1024
XBLK = 128 * NK * 33
OFF_INT = 0
OFF_OUTT = OFF_INT + NL * 16 * BLK
OFF_XT = OFF_OUTT + NL * 8 * BLK
OFF_OP1 = OFF_XT + NL * XBLK
OFF_OP2 = OFF_OP1 + 8 * BLK
OFF_FFN = OFF_OP2 + 8 * BLK
WSC_SZ = OFF_FFN + 8 * BLK


class Sem:
    def __init__(self, stk, nc, name):
        self.h = stk.enter_context(nc.semaphore(name))
        self.v = 0

    def reserve(self, k):
        self.v += k
        return self.v


class KB:
    ENGS = ("sync", "vector", "scalar", "tensor", "gpsimd")

    def __init__(self, nc):
        self.nc = nc
        self.script = []
        self.waited = {e: {} for e in self.ENGS}

    def em(self, eng, build, waits=(), inc=None, inck=1):
        real = []
        wd = self.waited[eng]
        for s, v in waits:
            if s is None or v <= 0:
                continue
            if wd.get(id(s), -1) >= v:
                continue
            wd[id(s)] = v
            real.append((s.h, v))
        if inc is not None:
            inc.reserve(inck)
        tgt = inc.v if inc is not None else None

        def fn(e, rw=real, build=build, inc=inc, inck=inck):
            for h, v in rw:
                e.wait_ge(h, v)
            ins = build(e)
            if inc is not None:
                ins.then_inc(inc.h, inck)

        self.script.append((eng, fn))
        return tgt

    def play(self, block):
        for eng_name in self.ENGS:
            deco = getattr(block, eng_name)
            ents = [fn for e, fn in self.script if e == eng_name]

            def body(e, ents=ents):
                for fn in ents:
                    fn(e)

            deco(body)


def build_program(n_layers=NL, do_epi=True, dbg_on=False):
    nc = bass.Bass("TRN2", target_bir_lowering=False)

    x0t = nc.dram_tensor("x0t", [H, S], F32, kind="ExternalInput").ap()
    winT = nc.dram_tensor("winT", [NL, H, 2048], F32, kind="ExternalInput").ap()
    woutT = nc.dram_tensor("woutT", [NL, H, H], F32, kind="ExternalInput").ap()
    wxT = nc.dram_tensor("wxT", [NL, H, 33], F32, kind="ExternalInput").ap()
    psml = nc.dram_tensor("psml", [NL * H, 16], F32, kind="ExternalInput").ap()
    phh = nc.dram_tensor("phh", [H, 12], F32, kind="ExternalInput").ap()
    pop = nc.dram_tensor("pop", [H, 4], F32, kind="ExternalInput").ap()
    wop1T = nc.dram_tensor("wop1T", [H, H], F32, kind="ExternalInput").ap()
    wop2T = nc.dram_tensor("wop2T", [H, H], F32, kind="ExternalInput").ap()
    wffn = nc.dram_tensor("wffn", [H, H], F32, kind="ExternalInput").ap()
    identf = nc.dram_tensor("identf", [128, 128], F32, kind="ExternalInput").ap()
    self16 = nc.dram_tensor("self16", [33, 33 * 128], F16, kind="ExternalInput").ap()
    out = nc.dram_tensor("out", [H, S], F32, kind="ExternalOutput").ap()
    dbg = None
    dbg2 = None
    if dbg_on:
        dbg = nc.dram_tensor("dbg", [n_layers + 2, H, SC], F32, kind="ExternalOutput").ap()
        dbg2 = nc.dram_tensor("dbg2", [8, H, SC], F32, kind="ExternalOutput").ap()

    wsc = nc.dram_tensor("wsc", [WSC_SZ], F16).ap()
    cc_prj_i = nc.dram_tensor("cc_prj_i", [33, SC], F32).ap()
    cc_prj_o = nc.dram_tensor("cc_prj_o", [33, SC], F32).ap()
    cc_st_i = nc.dram_tensor("cc_st_i", [1, 2 * SC], F32).ap()
    cc_st_o = nc.dram_tensor("cc_st_o", [1, 2 * SC], F32).ap()
    cc_dl_i = nc.dram_tensor("cc_dl_i", [H, SC], F16).ap()
    cc_dl_o = nc.dram_tensor("cc_dl_o", [H, SC], F16).ap()

    with contextlib.ExitStack() as stk:
        sbt = lambda n, sh, d: stk.enter_context(nc.sbuf_tensor(n, sh, d))
        pst_ = lambda n, sh, d: stk.enter_context(nc.psum_tensor(n, sh, d))

        x_sb = sbt("x_sb", [128, NK * SC], F32)
        xiab = sbt("xiab", [128, 2 * NK * (SC + 3)], F16)
        sres = sbt("sres", [128, NK * SC], F16)
        xn_sb = sbt("xn_sb", [128, NK * SC], F16)
        ar1 = sbt("ar1", [128, NK * SC], F32)
        warena = sbt("warena", [128, 3 * 1024], F16)
        u_t = sbt("u_t", [128, SC], F16)
        h4 = sbt("h4", [128, 4 * SC], F16)
        h_t = sbt("h_t", [128, SC], F16)
        Bbc = sbt("Bbc", [128, 2 * SC], F16)
        Cbc = sbt("Cbc", [128, 2 * SC], F16)
        rstd = sbt("rstd", [128, SC], F32)
        negm = sbt("negm", [128, SC], F32)
        scr1 = sbt("scr1", [128, SC], F32)
        scr2 = sbt("scr2", [128, SC], F32)
        scr3 = sbt("scr3", [128, SC], F32)
        scr4 = sbt("scr4", [128, SC], F32)
        psml_sb = sbt("psml_sb", [128, NL * NK * 16], F32)
        phh_sb = sbt("phh_sb", [128, NK * 12], F32)
        pop_sb = sbt("pop_sb", [128, NK * 4], F32)
        ident_sb = sbt("ident_sb", [128, 128], F32)
        ones32 = sbt("ones32", [128, 128], F32)
        ones16 = sbt("ones16", [128, 128], F16)
        sel_sb = sbt("sel_sb", [33, 33 * 128], F16)
        proj_sb = sbt("proj_sb", [33, SC], F32)
        projb = sbt("projb", [33, SC], F16)
        st_sb = sbt("st_sb", [128, NL * NK * NN], F16)
        halo = sbt("halo", [128, NL * NK * 2 * 3], F16)
        srow = sbt("srow", [1, 2 * SC], F32)
        onerow = sbt("onerow", [1, 128], F32)
        wq1 = sbt("wq1", [128, 32], F32)
        wq2 = sbt("wq2", [128, 32], F32)
        wq3 = sbt("wq3", [128, 32], F32)
        lncol = sbt("lncol", [128, 8], F32)
        cst = sbt("cst", [128, 24], F32)

        pm0 = pst_("pm0", [128, 512], F32)
        pm1 = pst_("pm1", [128, 512], F32)
        pb0 = pst_("pb0", [128, 1024], F32)
        pb1 = pst_("pb1", [128, 1024], F32)
        pstt = pst_("pstt", [128, 1024], F32)

        sw = Sem(stk, nc, "sw")
        sg = Sem(stk, nc, "sg")
        sp = Sem(stk, nc, "sp")
        sa = Sem(stk, nc, "sa")
        sv = Sem(stk, nc, "sv")
        sout = Sem(stk, nc, "sout")

        block = stk.enter_context(nc.Block())
        kb = KB(nc)

        def dmaS(dst, src, waits=()):
            return kb.em("sync", lambda e, d=dst, s=src: e.dma_start(out=d, in_=s),
                         waits=waits, inc=sw, inck=16)

        def dmaG(dst, src, waits=()):
            return kb.em("gpsimd", lambda e, d=dst, s=src: e.dma_start(out=d, in_=s),
                         waits=waits, inc=sg, inck=16)

        def xia(k):
            return xiab[:, k * (SC + 3) : (k + 1) * (SC + 3)]

        def xib(k):
            o = NK * (SC + 3)
            return xiab[:, o + k * (SC + 3) : o + (k + 1) * (SC + 3)]

        dt_v = ar1[:, 0 : NK * SC // 2].bitcast(F16)
        y_v = ar1[:, NK * SC // 2 : NK * SC].bitcast(F16)
        dtk = lambda k: dt_v[:, k * SC : (k + 1) * SC]
        yk = lambda k: y_v[:, k * SC : (k + 1) * SC]
        pcol = lambda l, k, c: psml_sb[:, (l * NK + k) * 16 + c : (l * NK + k) * 16 + c + 1]
        hcol = lambda k, c: phh_sb[:, k * 12 + c : k * 12 + c + 1]
        opcol = lambda k, c: pop_sb[:, k * 4 + c : k * 4 + c + 1]
        stcol = lambda l, k, n: st_sb[:, (l * NK + k) * NN + n : (l * NK + k) * NN + n + 1]
        halo3 = lambda l, k, cv: halo[:, ((l * NK + k) * 2 + cv) * 3 : ((l * NK + k) * 2 + cv) * 3 + 3]
        xt = [x_sb[:, k * SC : (k + 1) * SC] for k in range(NK)]

        G = {"pst": (None, 0), "pb0": (None, 0), "pb1": (None, 0),
             "pm0": (None, 0), "pm1": (None, 0)}

        dmaS(psml_sb[:, :].rearrange("p (l k c) -> p l k c", l=NL, k=NK), psml.rearrange("(l k p) c -> p l k c", p=128, k=NK))
        dmaS(phh_sb[:, :].rearrange("p (k c) -> p k c", k=NK), phh.rearrange("(k p) c -> p k c", p=128))
        dmaS(pop_sb[:, :].rearrange("p (k c) -> p k c", k=NK), pop.rearrange("(k p) c -> p k c", p=128))
        m_ident = dmaS(ident_sb[:, :], identf[:, :])
        m_sel = dmaS(sel_sb[:, :], self16[:, :])

        def _initv(e):
            e.memset(cst[:, 0:1], 3 * LN2)
            e.memset(cst[:, 1:2], -LN2)
            e.memset(cst[:, 2:3], -3 * LN2)
            e.memset(cst[:, 3:4], LN2)
            e.memset(cst[:, 4:5], -0.5)
            e.memset(cst[:, 5:6], 1e-5)
            e.memset(cst[:, 6:7], 1e-10)
            e.memset(cst[:, 7:8], 1.0 + 1e-5)
            for _n in range(16):
                e.memset(cst[:, 8 + _n : 9 + _n], -float(_n + 1))
            e.memset(ones32[:, :], 1.0)
            e.memset(ones16[:, :], 1.0)
            e.memset(onerow[:, :], 1.0)
            e.memset(st_sb[:, :], 0.0)
            e.memset(halo[:, :], 0.0)
            return e.memset(lncol[:, :], 0.0)

        m_initv = kb.em("vector", _initv, inc=sv)

        stage32 = x_sb[:, 0:1024]
        stage16v = xn_sb[:, 0:1024]
        last_cast = m_initv
        last_store = 0

        def repack(src2d, off, ncols):
            nonlocal last_cast, last_store
            n = NK * ncols
            m_in = dmaS(stage32[:, 0:n].rearrange("p (k c) -> p k c", k=NK),
                        src2d.rearrange("(k p) c -> p k c", p=128),
                        waits=[(sv, last_cast)])
            last_cast = kb.em("vector",
                              lambda e, n=n: e.tensor_copy(stage16v[:, 0:n], stage32[:, 0:n]),
                              waits=[(sw, max(m_in, last_store))], inc=sv)
            last_store = dmaS(wsc[off : off + 128 * n].rearrange("(p c) -> p c", p=128),
                              stage16v[:, 0:n], waits=[(sv, last_cast)])
            return last_store

        for l in range(NL):
            for m in range(16):
                repack(winT[l, :, m * 128 : (m + 1) * 128], OFF_INT + (l * 16 + m) * BLK, 128)
            for m in range(8):
                repack(woutT[l, :, m * 128 : (m + 1) * 128], OFF_OUTT + (l * 8 + m) * BLK, 128)
            repack(wxT[l, :, :], OFF_XT + l * XBLK, 33)
        for m in range(8):
            repack(wop1T[:, m * 128 : (m + 1) * 128], OFF_OP1 + m * BLK, 128)
            repack(wop2T[:, m * 128 : (m + 1) * 128], OFF_OP2 + m * BLK, 128)
        m_repack = last_store

        def quant_dequant(src, dst, waits0):
            r32 = lambda t: t.rearrange("p (w c) -> p w c", c=32)
            m_rd = kb.em("vector", lambda e: e.tensor_reduce(
                wq1[:, :].rearrange("p (w c) -> p w c", c=1), r32(src), AXX, ALU.max,
                apply_absolute_value=True), waits=waits0, inc=sv)
            m_cl = kb.em("gpsimd", lambda e: e.tensor_scalar(wq2[:, :], wq1[:, :], 1.0 / 7.5, 1e-30, ALU.mult, ALU.max),
                         waits=[(sv, m_rd)], inc=sg, inck=1)
            m_inv = kb.em("vector", lambda e: e.reciprocal(wq3[:, :], wq2[:, :]),
                          waits=[(sg, m_cl)], inc=sv)
            m_rl = kb.em("gpsimd", lambda e: e.tensor_copy(wq1[:, :], wq3[:, :]),
                         waits=[(sv, m_inv)], inc=sg, inck=1)
            bc3 = wq1[:, :].rearrange("p w -> p w ()").broadcast_to([128, 32, 32])
            kb.em("vector", lambda e: e.tensor_tensor(r32(scr3[:, :]), r32(src), bc3, ALU.mult),
                  waits=[(sg, m_rl)])
            kb.em("vector", lambda e: e.tensor_tensor(scr4[:, :], scr3[:, :], scr3[:, :], ALU.mult))
            kb.em("vector", lambda e: e.tensor_scalar(dst, scr4[:, :], 4.0, None, ALU.is_ge))
            kb.em("vector", lambda e: e.tensor_scalar(h_t[:, :], scr4[:, :], 16.0, None, ALU.is_ge))
            m_s = kb.em("vector", lambda e: e.tensor_tensor(dst, dst, h_t[:, :], ALU.add), inc=sv)
            m_si = kb.em("scalar", lambda e: e.activation(scr4[:, :], dst, AF.Exp, bias=cst[:, 0:1], scale=cst[:, 1:2]),
                         waits=[(sv, m_s)], inc=sa)
            m_st = kb.em("scalar", lambda e: e.activation(dst, dst, AF.Exp, bias=cst[:, 2:3], scale=cst[:, 3:4]),
                         inc=sa)
            m_w = kb.em("vector", lambda e: e.tensor_tensor(scr3[:, :], scr3[:, :], scr4[:, :], ALU.mult),
                        waits=[(sa, m_si)], inc=sv)
            kb.em("vector", lambda e: e.tensor_copy(scr4[:, :].bitcast(I32)[:, 0:SC], scr3[:, :]))
            m_r = kb.em("vector", lambda e: e.tensor_copy(scr3[:, :], scr4[:, :].bitcast(I32)[:, 0:SC]), inc=sv)
            kb.em("vector", lambda e: e.tensor_tensor(dst, scr3[:, :], dst, ALU.mult),
                  waits=[(sa, m_st), (sv, m_r)])
            bc2 = wq2[:, :].rearrange("p w -> p w ()").broadcast_to([128, 32, 32])
            m_q = kb.em("vector", lambda e: e.tensor_tensor(r32(dst), r32(dst), bc2, ALU.mult), inc=sv)
            return m_q

        for i in range(8):
            m_in = dmaS(scr2[:, :], wffn[i * 128 : (i + 1) * 128, :], waits=[(sv, last_cast)])
            mq = quant_dequant(scr2[:, :], scr1[:, :], [(sw, m_in), (sv, m_initv)])
            m_ev = 0
            for j in range(8):
                pmt = pm0 if j % 2 == 0 else pm1
                gk = "pm0" if j % 2 == 0 else "pm1"
                m_tr = kb.em("tensor", lambda e, j=j, pmt=pmt: e.transpose(
                    pmt[:, 0:128], scr1[:, j * 128 : (j + 1) * 128], ident_sb[:, :]),
                    waits=[(sv, mq), (sw, m_ident), G[gk]], inc=sp)
                m_ev = kb.em("scalar", lambda e, j=j, pmt=pmt: e.copy(
                    stage16v[:, j * 128 : (j + 1) * 128], pmt[:, 0:128]),
                    waits=[(sp, m_tr), (sw, m_repack)], inc=sa)
                G[gk] = (sa, m_ev)
            dmaS(wsc[OFF_FFN + i * BLK : OFF_FFN + (i + 1) * BLK].rearrange("(p c) -> p c", p=128),
                 stage16v[:, 0:1024], waits=[(sa, m_ev)])
        m_setup = sw.v

        ring_free = [(None, 0)] * 3
        ring_state = [0]

        def ring_load(off, ncols=1024):
            sl = ring_state[0] % 3
            ring_state[0] += 1
            g = ring_free[sl]
            slot = warena[:, sl * 1024 : sl * 1024 + ncols]
            m = dmaS(slot, wsc[off : off + 128 * ncols].rearrange("(p c) -> p c", p=128),
                     waits=[g, (sw, m_setup)])
            return sl, slot, m

        def ln_stats2(tiles, rstd_t, negm_t, dep, src_f16=False, lnb=None, lns=None):
            rstd_t = rstd_t[:, :]
            negm_t = negm_t[:, :]
            ones_t = ones16 if src_f16 else ones32
            m_mm = 0
            for nh in range(2):
                for k in range(NK):
                    first = (k == 0 and nh == 0)
                    m_mm = kb.em("tensor", lambda e, k=k, nh=nh, ot=ones_t: e.matmul(
                        pstt[:, nh * 512 : (nh + 1) * 512], ot[:, :],
                        tiles[k][:, nh * 512 : (nh + 1) * 512],
                        start=(k == 0), stop=(k == NK - 1)),
                        waits=([dep, G["pst"]] if first else [dep]),
                        inc=(sp if (k == NK - 1 and nh == 1) else None))
            m_mu = kb.em("vector", lambda e, nt=negm_t: e.tensor_scalar(nt, pstt[:, :], 1.0 / 2048.0 if src_f16 else 1.0 / H, None, ALU.mult),
                         waits=[(sp, m_mm)], inc=sv)
            sq_guard = [0, 0, 0, 0]
            m_mm2 = 0
            for k in range(NK):
                d = (scr1, scr2, scr3, scr4)[k % 4]
                dv = d[:, :].bitcast(F16)[:, 0:SC] if src_f16 else d[:, :]
                w = [(sp, sq_guard[k % 4])] if sq_guard[k % 4] else [dep]
                m_sq = kb.em("scalar", lambda e, k=k, dv=dv: e.activation(dv, tiles[k][:, :], AF.Square),
                             waits=w, inc=sa)
                for nh in range(2):
                    m_mm2 = kb.em("tensor", lambda e, k=k, nh=nh, dv=dv, ot=ones_t: e.matmul(
                        pstt[:, nh * 512 : (nh + 1) * 512], ot[:, :],
                        dv[:, nh * 512 : (nh + 1) * 512],
                        start=(k == 0), stop=(k == NK - 1)),
                        waits=[(sa, m_sq)] + ([(sv, m_mu)] if (k == 0 and nh == 0) else []),
                        inc=sp)
                sq_guard[k % 4] = m_mm2
            m_v1 = kb.em("vector", lambda e: e.tensor_scalar(scr3[:, :], pstt[:, :], 1.0 / 2048.0 if src_f16 else 1.0 / H, None, ALU.mult),
                         waits=[(sp, m_mm2)], inc=sv)
            kb.em("vector", lambda e, nt=negm_t: e.tensor_tensor(scr4[:, :], nt, nt, ALU.mult))
            m_v2 = kb.em("vector", lambda e: e.tensor_tensor(scr3[:, :], scr3[:, :], scr4[:, :], ALU.subtract), inc=sv)
            G["pst"] = (sv, m_v1)
            _lnb = lnb if lnb is not None else cst[:, 5:6]
            if lns is None:
                m_l = kb.em("scalar", lambda e, _lnb=_lnb: e.activation(scr3[:, :], scr3[:, :], AF.Ln, bias=_lnb),
                            waits=[(sv, m_v2)], inc=sa)
            else:
                m_l = kb.em("scalar", lambda e, _lnb=_lnb, lns=lns: e.activation(scr3[:, :], scr3[:, :], AF.Ln, bias=_lnb, scale=lns),
                            waits=[(sv, m_v2)], inc=sa)
            kb.em("scalar", lambda e, rt=rstd_t: e.activation(rt, scr3[:, :], AF.Exp, scale=cst[:, 4:5]), inc=sa)
            m_n = kb.em("vector", lambda e, nt=negm_t, rt=rstd_t: e.scalar_tensor_tensor(
                nt, nt, -1.0, rt, ALU.mult, ALU.mult), waits=[(sa, sa.v)], inc=sv)
            return m_n

        last_out = 0

        for c in range(NCH):
            m_x0 = dmaS(x_sb[:, :].rearrange("p (k t) -> p k t", k=NK),
                        x0t[:, c * SC : (c + 1) * SC].rearrange("(k p) t -> p k t", p=128),
                        waits=[(sv, sv.v), (sa, sa.v), (sp, sp.v), (sg, sg.v), (sout, last_out)])
            m_n2 = ln_stats2(xt, rstd, negm, (sw, m_x0), lnb=cst[:, 6:7], lns=cst[:, 7:8])
            x_dep = None
            m_xr = 0
            for k in range(NK):
                kb.em("vector", lambda e, k=k: e.tensor_tensor(scr1[:, :], xt[k], rstd[:, :], ALU.mult),
                      waits=[(sv, m_n2)])
                m_s2 = kb.em("vector", lambda e: e.tensor_tensor(scr2[:, :], scr1[:, :], negm[:, :], ALU.add),
                             waits=[(sa, m_xr)], inc=sv)
                m_xr = kb.em("scalar", lambda e, k=k: e.activation(
                    xt[k], scr2[:, :], AF.Identity, bias=hcol(k, 9), scale=hcol(k, 8)),
                    waits=[(sv, m_s2)], inc=sa)
            x_dep = (sa, m_xr)

            if dbg is not None and c == 0:
                for k in range(NK):
                    dmaS(dbg[0, k * 128 : (k + 1) * 128, :], xt[k], waits=[x_dep])

            for l in range(n_layers):
                m_n3 = ln_stats2(xt, rstd, negm, x_dep)
                m_xn = 0
                for k in range(NK):
                    kb.em("vector", lambda e, k=k: e.tensor_tensor(scr1[:, :], xt[k], rstd[:, :], ALU.mult),
                          waits=[(sv, m_n3)])
                    m_s2 = kb.em("vector", lambda e: e.tensor_tensor(scr2[:, :], scr1[:, :], negm[:, :], ALU.add),
                                 waits=[(sa, m_xn)], inc=sv)
                    m_xn = kb.em("scalar", lambda e, k=k, l=l: e.activation(
                        xn_sb[:, k * SC : (k + 1) * SC], scr2[:, :], AF.Identity,
                        bias=hcol(k, 4 + l), scale=hcol(k, l)),
                        waits=[(sv, m_s2)], inc=sa)

                if dbg2 is not None and c == 0 and l == -1:
                    for k in range(NK):
                        kb.em("gpsimd", lambda e, k=k: e.dma_start(
                            out=dbg2[0, k * 128 : (k + 1) * 128, :], in_=xn_sb[:, k * SC : (k + 1) * SC]),
                            waits=[(sa, m_xn)], inc=sg, inck=16)
                for k in range(NK):
                    kb.em("vector", lambda e, k=k, l=l: e.tensor_copy(xia(k)[:, 0:3], halo3(l, k, 0)))
                    kb.em("vector", lambda e, k=k, l=l: e.tensor_copy(xib(k)[:, 0:3], halo3(l, k, 1)),
                          inc=(sv if k == NK - 1 else None))
                m_halo = sv.v

                for m in range(16):
                    sl, slot, m_w = ring_load(OFF_INT + (l * 16 + m) * BLK)
                    m_mm_last = 0
                    for nh in range(2):
                        bi = nh
                        pmt = pm0 if bi == 0 else pm1
                        gk = "pm0" if bi == 0 else "pm1"
                        m_mm = 0
                        for k in range(NK):
                            m_mm = kb.em("tensor", lambda e, k=k, nh=nh, pmt=pmt, slot=slot: e.matmul(
                                pmt[:, :], slot[:, k * 128 : (k + 1) * 128],
                                xn_sb[:, k * SC + nh * 512 : k * SC + (nh + 1) * 512],
                                start=(k == 0), stop=(k == NK - 1)),
                                waits=[(sa, m_xn), (sw, m_w)] + ([G[gk]] if k == 0 else []),
                                inc=(sp if k == NK - 1 else None))
                        if m < 8:
                            dst = xia(m)[:, 3 + nh * 512 : 3 + (nh + 1) * 512]
                            fn = AF.Silu
                        else:
                            dst = sres[:, (m - 8) * SC + nh * 512 : (m - 8) * SC + (nh + 1) * 512]
                            fn = AF.Sigmoid
                        m_ev = kb.em("scalar", lambda e, dst=dst, fn=fn, pmt=pmt: e.activation(dst, pmt[:, :], fn),
                                     waits=[(sp, m_mm), (sv, m_halo)], inc=sa)
                        G[gk] = (sa, m_ev)
                        m_mm_last = m_mm
                    ring_free[sl] = (sp, m_mm_last)
                m_evin = sa.v

                m_xi = 0
                for k in range(NK):
                    s3f = scr3[:, :].bitcast(F16)[:, 0:SC]
                    s4f = scr4[:, :].bitcast(F16)[:, 0:SC]
                    kb.em("vector", lambda e, k=k, l=l, s3f=s3f: e.tensor_scalar(
                        s3f, xia(k)[:, 3 : 3 + SC], pcol(l, k, PC_CW0 + 3), None, ALU.mult),
                        waits=[(sa, m_evin), (sa, m_xi)])
                    for j in range(3):
                        kb.em("vector", lambda e, k=k, l=l, j=j, s4f=s4f: e.tensor_scalar(
                            s4f, xia(k)[:, j : j + SC], pcol(l, k, PC_CW0 + j), None, ALU.mult))
                        kb.em("vector", lambda e, s3f=s3f, s4f=s4f: e.tensor_tensor(s3f, s3f, s4f, ALU.add))
                    m_c1 = kb.em("vector", lambda e, k=k, l=l: e.tensor_copy(halo3(l, k, 0), xia(k)[:, SC : SC + 3]),
                                 inc=sv)
                    m_c1a = kb.em("scalar", lambda e, k=k, l=l, s3f=s3f: e.activation(
                        xib(k)[:, 3 : 3 + SC], s3f, AF.Silu, bias=pcol(l, k, PC_CB0)),
                        waits=[(sv, m_c1)], inc=sa)
                    kb.em("vector", lambda e, k=k, l=l, s3f=s3f: e.tensor_scalar(
                        s3f, xib(k)[:, 3 : 3 + SC], pcol(l, k, PC_CW1 + 3), None, ALU.mult),
                        waits=[(sa, m_c1a)])
                    for j in range(3):
                        kb.em("vector", lambda e, k=k, l=l, j=j, s4f=s4f: e.tensor_scalar(
                            s4f, xib(k)[:, j : j + SC], pcol(l, k, PC_CW1 + j), None, ALU.mult))
                        kb.em("vector", lambda e, s3f=s3f, s4f=s4f: e.tensor_tensor(s3f, s3f, s4f, ALU.add))
                    m_c2 = kb.em("vector", lambda e, k=k, l=l: e.tensor_copy(halo3(l, k, 1), xib(k)[:, SC : SC + 3]),
                                 inc=sv)
                    m_xi = kb.em("scalar", lambda e, k=k, l=l, s3f=s3f: e.activation(
                        xia(k)[:, 3 : 3 + SC], s3f, AF.Silu, bias=pcol(l, k, PC_CB1)),
                        waits=[(sv, m_c2)], inc=sa)

                if dbg2 is not None and c == 0 and l == -1:
                    for k in range(NK):
                        kb.em("gpsimd", lambda e, k=k: e.dma_start(
                            out=dbg2[1, k * 128 : (k + 1) * 128, :], in_=xia(k)[:, 3 : 3 + SC]),
                            waits=[(sa, m_xi)], inc=sg, inck=16)
                        kb.em("gpsimd", lambda e, k=k: e.dma_start(
                            out=dbg2[6, k * 128 : (k + 1) * 128, :], in_=xib(k)[:, 3 : 3 + SC]),
                            waits=[(sa, m_xi)], inc=sg, inck=16)
                        kb.em("gpsimd", lambda e, k=k: e.dma_start(
                            out=dbg2[2, k * 128 : (k + 1) * 128, :], in_=sres[:, k * SC : (k + 1) * SC]),
                            waits=[(sa, m_xi)], inc=sg, inck=16)
                m_px = 0
                sl, slotx, m_wx = ring_load(OFF_XT + l * XBLK, NK * 33)
                m_mm_x = 0
                for nh in range(2):
                    pmt = pm0 if nh == 0 else pm1
                    gk = "pm0" if nh == 0 else "pm1"
                    m_mm = 0
                    for k in range(NK):
                        m_mm = kb.em("tensor", lambda e, k=k, nh=nh, pmt=pmt, slotx=slotx: e.matmul(
                            pmt[0:33, :], slotx[:, k * 33 : (k + 1) * 33],
                            xia(k)[:, 3 + nh * 512 : 3 + (nh + 1) * 512],
                            start=(k == 0), stop=(k == NK - 1)),
                            waits=[(sa, m_xi), (sw, m_wx)] + ([G[gk]] if k == 0 else []),
                            inc=(sp if k == NK - 1 else None))
                    m_px = kb.em("vector", lambda e, nh=nh, pmt=pmt: e.tensor_copy(
                        proj_sb[0:33, nh * 512 : (nh + 1) * 512], pmt[0:33, :]),
                        waits=[(sp, m_mm)], inc=sv)
                    G[gk] = (sv, m_px)
                    m_mm_x = m_mm
                ring_free[sl] = (sp, m_mm_x)
                m_yi = 0
                for k in range(NK):
                    m_yi = kb.em("vector", lambda e, k=k, l=l: e.tensor_scalar(
                        yk(k), xia(k)[:, 3 : 3 + SC], pcol(l, k, PC_D), None, ALU.mult),
                        waits=[(sa, m_xi)], inc=sv)
                m_gi = dmaG(cc_prj_i[:, :], proj_sb[:, :], waits=[(sv, m_px)])
                m_cc1 = kb.em("gpsimd", lambda e: e.collective_compute(
                    "AllReduce", ALU.add, ins=[cc_prj_i[:, :].opt()], outs=[cc_prj_o[:, :].opt()],
                    replica_groups=GROUPS), waits=[(sg, m_gi)], inc=sg, inck=1)
                m_gb = dmaG(proj_sb[:, :], cc_prj_o[:, :], waits=[(sg, m_cc1)])
                m_pjb = kb.em("vector", lambda e: e.tensor_copy(projb[:, :], proj_sb[:, :]),
                              waits=[(sg, m_gb)], inc=sv)

                m_dtbc = 0
                for nh in range(2):
                    m_dtbc = kb.em("tensor", lambda e, nh=nh: e.matmul(
                        pb0[:, nh * 512 : (nh + 1) * 512], sel_sb[:, 0:128],
                        projb[0:33, nh * 512 : (nh + 1) * 512], start=True, stop=True),
                        waits=[(sv, m_pjb), (sw, m_sel)] + ([G["pb0"]] if nh == 0 else []),
                        inc=sp)
                m_dt = 0
                m_ab = 0
                for k in range(NK):
                    m_ts = kb.em("vector", lambda e, k=k, l=l: e.tensor_scalar(
                        scr1[:, :], pb0[:, :], pcol(l, k, PC_DTW), pcol(l, k, PC_DTB), ALU.mult, ALU.add),
                        waits=[(sp, m_dtbc), (sa, m_ab)], inc=sv)
                    m_ab = kb.em("scalar", lambda e: e.activation(scr2[:, :], scr1[:, :], AF.Abs),
                                 waits=[(sv, max(m_ts, m_dt))], inc=sa)
                    m_e = kb.em("scalar", lambda e: e.activation(scr3[:, :], scr2[:, :], AF.Exp, scale=-1.0),
                                inc=sa)
                    m_l1 = kb.em("scalar", lambda e: e.activation(scr2[:, :], scr3[:, :], AF.Ln, bias=1.0), inc=sa)
                    kb.em("vector", lambda e: e.tensor_scalar(scr4[:, :], scr1[:, :], 0.0, None, ALU.max))
                    m_dt = kb.em("vector", lambda e, k=k: e.tensor_tensor(dtk(k), scr2[:, :], scr4[:, :], ALU.add),
                                 waits=[(sa, m_l1)], inc=sv)
                G["pb0"] = (sv, m_dt)

                if dbg2 is not None and c == 0 and l == -1:
                    kb.em("gpsimd", lambda e: e.dma_start(
                        out=dbg2[7, 0:33, :], in_=proj_sb[:, :]),
                        waits=[(sv, m_pjb)], inc=sg, inck=16)
                    for k in range(NK):
                        kb.em("gpsimd", lambda e, k=k: e.dma_start(
                            out=dbg2[3, k * 128 : (k + 1) * 128, :], in_=dtk(k)),
                            waits=[(sv, m_dt)], inc=sg, inck=16)
                m_dtx = 0
                for k in range(NK):
                    m_dtx = kb.em("vector", lambda e, k=k: e.tensor_tensor(
                        xib(k)[:, 3 : 3 + SC], dtk(k), xia(k)[:, 3 : 3 + SC], ALU.mult), inc=sv)
                m_scan_hist = [m_dtx, m_dtx]
                m_scan = m_dtx
                m_ev = [[0, 0] for _ in range(NN)]

                def emit_bcast(n):
                    mB = 0
                    for nh in range(2):
                        mB = kb.em("tensor", lambda e, n=n, nh=nh: e.matmul(
                            pb0[:, nh * 512 : (nh + 1) * 512], sel_sb[:, (1 + n) * 128 : (2 + n) * 128],
                            projb[0:33, nh * 512 : (nh + 1) * 512], start=True, stop=True),
                            waits=([G["pb0"]] if nh == 0 else []), inc=sp)
                    mC = 0
                    for nh in range(2):
                        mC = kb.em("tensor", lambda e, n=n, nh=nh: e.matmul(
                            pb1[:, nh * 512 : (nh + 1) * 512], sel_sb[:, (17 + n) * 128 : (18 + n) * 128],
                            projb[0:33, nh * 512 : (nh + 1) * 512], start=True, stop=True),
                            waits=([G["pb1"]] if nh == 0 else []), inc=sp)
                    sl = n % 2
                    m_evB = kb.em("scalar", lambda e, sl=sl: e.copy(Bbc[:, sl * SC : (sl + 1) * SC], pb0[:, :]),
                                  waits=[(sp, mB), (sv, m_scan_hist[0])], inc=sa)
                    m_evC = kb.em("scalar", lambda e, sl=sl: e.copy(Cbc[:, sl * SC : (sl + 1) * SC], pb1[:, :]),
                                  waits=[(sp, mC), (sv, m_scan_hist[0])], inc=sa)
                    G["pb0"] = (sa, m_evB)
                    G["pb1"] = (sa, m_evC)
                    m_ev[n] = [m_evB, m_evC]

                a_slots = [scr1, scr2, scr3, scr4]
                sc_guard = [m_dtx, m_dtx, m_dtx, m_dtx]
                emit_bcast(0)
                for n in range(NN):
                    if n + 1 < NN:
                        emit_bcast(n + 1)
                    sl = n % 2
                    m_evB, m_evC = m_ev[n]
                    for k in range(NK):
                        s4 = (n * NK + k) % 4
                        at = a_slots[s4]
                        hsl = h4[:, s4 * SC : (s4 + 1) * SC]
                        m_ag = kb.em("scalar", lambda e, k=k, n=n, at=at: e.activation(
                            at[:, :], dtk(k), AF.Identity, bias=1.0, scale=cst[:, 8 + n : 9 + n]),
                            waits=[(sv, sc_guard[s4]), (sv, m_dt)], inc=sa)
                        kb.em("vector", lambda e, k=k, sl=sl: e.tensor_tensor(
                            u_t[:, :], xib(k)[:, 3 : 3 + SC], Bbc[:, sl * SC : (sl + 1) * SC], ALU.mult),
                            waits=[(sa, m_evB), (sa, m_evC)])
                        m_sc = kb.em("vector", lambda e, k=k, n=n, l=l, at=at, hsl=hsl: e.tensor_tensor_scan(
                            hsl, at[:, :], u_t[:, :], stcol(l, k, n), ALU.mult, ALU.add),
                            waits=[(sa, m_ag)], inc=sv)
                        sc_guard[s4] = m_sc
                        kb.em("vector", lambda e, sl=sl, hsl=hsl: e.tensor_tensor(
                            u_t[:, :], hsl, Cbc[:, sl * SC : (sl + 1) * SC], ALU.mult))
                        m_scan = kb.em("vector", lambda e, k=k: e.tensor_tensor(
                            yk(k), yk(k), u_t[:, :], ALU.add), inc=sv)
                        if s4 == 3:
                            k0 = k - 3
                            dst = st_sb[:, :].rearrange("p (x n) -> p x n", n=NN)[
                                :, l * NK + k0 : l * NK + k0 + 4, n : n + 1]
                            src = h4[:, :].rearrange("p (s t) -> p s t", s=4)[:, :, SC - 1 : SC]
                            m_scan = kb.em("vector", lambda e, dst=dst, src=src: e.tensor_copy(dst, src),
                                           inc=sv)
                    m_scan_hist = [m_scan, m_scan_hist[0]]

                m_ys = 0
                for nh in range(2):
                    for k in range(NK):
                        first = (k == 0 and nh == 0)
                        m_ys = kb.em("tensor", lambda e, k=k, nh=nh: e.matmul(
                            pstt[:, nh * 512 : (nh + 1) * 512], ones16[:, :],
                            y_v[:, k * SC + nh * 512 : k * SC + (nh + 1) * 512],
                            start=(k == 0), stop=(k == NK - 1)),
                            waits=[(sv, m_scan)] + ([G["pst"]] if first else []),
                            inc=(sp if (k == NK - 1 and nh == 1) else None))
                m_r1 = kb.em("vector", lambda e: e.tensor_copy(srow[:, 0:SC], pstt[0:1, :]),
                             waits=[(sp, m_ys)], inc=sv)
                sq_guard = [0, 0, 0, 0]
                m_yq = 0
                for k in range(NK):
                    d = (scr1, scr2, scr3, scr4)[k % 4]
                    dv = d[:, :].bitcast(F16)[:, 0:SC]
                    w = [(sp, sq_guard[k % 4])] if sq_guard[k % 4] else [(sv, m_scan)]
                    m_sq = kb.em("scalar", lambda e, k=k, dv=dv: e.activation(dv, yk(k), AF.Square),
                                 waits=w, inc=sa)
                    for nh in range(2):
                        m_yq = kb.em("tensor", lambda e, k=k, nh=nh, dv=dv: e.matmul(
                            pstt[:, nh * 512 : (nh + 1) * 512], ones16[:, :],
                            dv[:, nh * 512 : (nh + 1) * 512],
                            start=(k == 0), stop=(k == NK - 1)),
                            waits=[(sa, m_sq)] + ([(sv, m_r1)] if (k == 0 and nh == 0) else []),
                            inc=sp)
                    sq_guard[k % 4] = m_yq
                m_r2 = kb.em("vector", lambda e: e.tensor_copy(srow[:, SC : 2 * SC], pstt[0:1, :]),
                             waits=[(sp, m_yq)], inc=sv)
                G["pst"] = (sv, m_r2)
                m_gi2 = dmaG(cc_st_i[:, :], srow[:, :], waits=[(sv, m_r2)])
                m_cc2 = kb.em("gpsimd", lambda e: e.collective_compute(
                    "AllReduce", ALU.add, ins=[cc_st_i[:, :].opt()], outs=[cc_st_o[:, :].opt()],
                    replica_groups=GROUPS), waits=[(sg, m_gi2)], inc=sg, inck=1)
                m_gb2 = dmaG(srow[:, :], cc_st_o[:, :], waits=[(sg, m_cc2)])
                m_bc = 0
                for half in range(2):
                    pbt = pb0 if half == 0 else pb1
                    gk = "pb0" if half == 0 else "pb1"
                    for nh in range(2):
                        m_bc = kb.em("tensor", lambda e, half=half, nh=nh, pbt=pbt: e.matmul(
                            pbt[:, nh * 512 : (nh + 1) * 512], onerow[:, :],
                            srow[:, half * SC + nh * 512 : half * SC + (nh + 1) * 512],
                            start=True, stop=True),
                            waits=[(sg, m_gb2)] + ([G[gk]] if nh == 0 else []), inc=sp)
                DIT = 2048.0
                m_mu = kb.em("vector", lambda e: e.tensor_scalar(negm[:, :], pb0[:, :], 1.0 / DIT, None, ALU.mult),
                             waits=[(sp, m_bc)], inc=sv)
                kb.em("vector", lambda e: e.tensor_scalar(scr1[:, :], pb1[:, :], 1.0 / DIT, None, ALU.mult))
                kb.em("vector", lambda e: e.tensor_tensor(scr2[:, :], negm[:, :], negm[:, :], ALU.mult))
                m_v2 = kb.em("vector", lambda e: e.tensor_tensor(scr1[:, :], scr1[:, :], scr2[:, :], ALU.subtract),
                             inc=sv)
                G["pb0"] = (sv, m_v2)
                G["pb1"] = (sv, m_v2)
                m_l2 = kb.em("scalar", lambda e: e.activation(scr1[:, :], scr1[:, :], AF.Ln, bias=cst[:, 5:6]),
                             waits=[(sv, m_v2)], inc=sa)
                m_r3 = kb.em("scalar", lambda e: e.activation(rstd[:, :], scr1[:, :], AF.Exp, scale=cst[:, 4:5]), inc=sa)
                m_nm = kb.em("vector", lambda e: e.scalar_tensor_tensor(
                    negm[:, :], negm[:, :], -1.0, rstd[:, :], ALU.mult, ALU.mult),
                    waits=[(sa, m_r3)], inc=sv)

                m_yh = 0
                m_g = 0
                for k in range(NK):
                    s3f = scr3[:, :].bitcast(F16)[:, 0:SC]
                    kb.em("vector", lambda e, k=k: e.tensor_tensor(scr1[:, :], yk(k), rstd[:, :], ALU.mult),
                          waits=[(sv, m_nm), (sa, m_g)])
                    m_a = kb.em("vector", lambda e: e.tensor_tensor(scr2[:, :], scr1[:, :], negm[:, :], ALU.add),
                                inc=sv)
                    m_g = kb.em("scalar", lambda e, k=k, l=l, s3f=s3f: e.activation(
                        s3f, scr2[:, :], AF.Identity, bias=pcol(l, k, PC_MB), scale=pcol(l, k, PC_MG)),
                        waits=[(sv, m_a)], inc=sa)
                    m_yh = kb.em("vector", lambda e, k=k, s3f=s3f: e.tensor_tensor(
                        xia(k)[:, 3 : 3 + SC], s3f, sres[:, k * SC : (k + 1) * SC], ALU.mult),
                        waits=[(sa, m_g)], inc=sv)

                if dbg2 is not None and c == 0 and l == -1:
                    for k in range(NK):
                        kb.em("gpsimd", lambda e, k=k: e.dma_start(
                            out=dbg2[5, k * 128 : (k + 1) * 128, :], in_=xia(k)[:, 3 : 3 + SC]),
                            waits=[(sv, m_yh)], inc=sg, inck=16)
                dlv = xn_sb
                dl16 = dlv[:, :]
                m_dl_half = 0
                m_dl_dma = 0
                for m in range(8):
                    sl, slot, m_w = ring_load(OFF_OUTT + (l * 8 + m) * BLK)
                    m_evs = []
                    m_mm_last = 0
                    for nh in range(2):
                        pmt = pm0 if nh == 0 else pm1
                        gk = "pm0" if nh == 0 else "pm1"
                        m_mm = 0
                        for k in range(NK):
                            m_mm = kb.em("tensor", lambda e, k=k, nh=nh, pmt=pmt, slot=slot: e.matmul(
                                pmt[:, :], slot[:, k * 128 : (k + 1) * 128],
                                xia(k)[:, 3 + nh * 512 : 3 + (nh + 1) * 512],
                                start=(k == 0), stop=(k == NK - 1)),
                                waits=[(sv, m_yh), (sw, m_w)] + ([G[gk]] if k == 0 else []),
                                inc=(sp if k == NK - 1 else None))
                        m_ev = kb.em("vector", lambda e, pmt=pmt, m=m, nh=nh: e.tensor_copy(
                            dl16[:, m * SC + nh * 512 : m * SC + (nh + 1) * 512], pmt[:, :]),
                            waits=[(sp, m_mm)], inc=sv)
                        G[gk] = (sv, m_ev)
                        m_evs.append(m_ev)
                        m_mm_last = m_mm
                    ring_free[sl] = (sp, m_mm_last)
                    if m == 3:
                        m_dl_half = dmaG(cc_dl_i[0:512, :].rearrange("(m p) t -> p m t", p=128),
                                         dl16[:, 0 : 4 * SC].rearrange("p (m t) -> p m t", m=4),
                                         waits=[(sv, max(m_evs))])
                    if m == 7:
                        m_dl_dma = dmaG(cc_dl_i[512:1024, :].rearrange("(m p) t -> p m t", p=128),
                                        dl16[:, 4 * SC : 8 * SC].rearrange("p (m t) -> p m t", m=4),
                                        waits=[(sv, max(m_evs))])
                m_cc3a = kb.em("gpsimd", lambda e: e.collective_compute(
                    "AllReduce", ALU.add, ins=[cc_dl_i[0:512, :].opt()], outs=[cc_dl_o[0:512, :].opt()],
                    replica_groups=GROUPS), waits=[(sg, m_dl_half)], inc=sg, inck=1)
                m_bk1 = dmaG(dl16[:, 0 : 4 * SC].rearrange("p (m t) -> p m t", m=4),
                             cc_dl_o[0:512, :].rearrange("(m p) t -> p m t", p=128),
                             waits=[(sg, m_cc3a)])
                m_cc3b = kb.em("gpsimd", lambda e: e.collective_compute(
                    "AllReduce", ALU.add, ins=[cc_dl_i[512:1024, :].opt()], outs=[cc_dl_o[512:1024, :].opt()],
                    replica_groups=GROUPS), waits=[(sg, m_dl_dma)], inc=sg, inck=1)
                m_bk2 = dmaG(dl16[:, 4 * SC : 8 * SC].rearrange("p (m t) -> p m t", m=4),
                             cc_dl_o[512:1024, :].rearrange("(m p) t -> p m t", p=128),
                             waits=[(sg, m_cc3b)])
                kb.em("vector", lambda e: e.tensor_tensor(
                    x_sb[:, 0 : 4 * SC], x_sb[:, 0 : 4 * SC], dl16[:, 0 : 4 * SC], ALU.add),
                    waits=[(sg, m_bk1)])
                m_xu = kb.em("vector", lambda e: e.tensor_tensor(
                    x_sb[:, 4 * SC : 8 * SC], x_sb[:, 4 * SC : 8 * SC], dl16[:, 4 * SC : 8 * SC], ALU.add),
                    waits=[(sg, m_bk2)], inc=sv)
                x_dep = (sv, m_xu)
                if dbg is not None and c == 0:
                    for k in range(NK):
                        dmaS(dbg[l + 1, k * 128 : (k + 1) * 128, :], xt[k], waits=[x_dep])

            if do_epi:
                m_xfb = 0
                for k in range(NK):
                    m_xfb = kb.em("vector", lambda e, k=k: e.tensor_copy(
                        xn_sb[:, k * SC : (k + 1) * SC], xt[k]), waits=[x_dep], inc=sv)
                z1 = sres
                for m in range(8):
                    sl, slot, m_w = ring_load(OFF_OP1 + m * BLK)
                    m_mm_last = 0
                    for nh in range(2):
                        pmt = pm0 if nh == 0 else pm1
                        gk = "pm0" if nh == 0 else "pm1"
                        m_mm = 0
                        for k in range(NK):
                            m_mm = kb.em("tensor", lambda e, k=k, nh=nh, pmt=pmt, slot=slot: e.matmul(
                                pmt[:, :], slot[:, k * 128 : (k + 1) * 128],
                                xn_sb[:, k * SC + nh * 512 : k * SC + (nh + 1) * 512],
                                start=(k == 0), stop=(k == NK - 1)),
                                waits=[(sv, m_xfb), (sw, m_w)] + ([G[gk]] if k == 0 else []),
                                inc=(sp if k == NK - 1 else None))
                        m_ev = kb.em("scalar", lambda e, m=m, nh=nh, pmt=pmt: e.activation(
                            z1[:, m * SC + nh * 512 : m * SC + (nh + 1) * 512], pmt[:, :],
                            AF.Gelu, bias=opcol(m, 0)),
                            waits=[(sp, m_mm)], inc=sa)
                        G[gk] = (sa, m_ev)
                        m_mm_last = m_mm
                    ring_free[sl] = (sp, m_mm_last)
                m_z1 = sa.v
                m_x0b = dmaS(ar1[:, :].rearrange("p (k t) -> p k t", k=NK),
                             x0t[:, c * SC : (c + 1) * SC].rearrange("(k p) t -> p k t", p=128),
                             waits=[(sv, sv.v), (sa, sa.v)])
                x0l = [ar1[:, k * SC : (k + 1) * SC] for k in range(NK)]
                m_n0 = ln_stats2(x0l, rstd, negm, (sw, m_x0b))
                aov = xiab[:, :].bitcast(F32)
                ao = [aov[:, k * SC : (k + 1) * SC] for k in range(NK)]
                m_ao = 0
                for m in range(8):
                    sl, slot, m_w = ring_load(OFF_OP2 + m * BLK)
                    m_mm_last = 0
                    for nh in range(2):
                        pmt = pm0 if nh == 0 else pm1
                        gk = "pm0" if nh == 0 else "pm1"
                        m_mm = 0
                        for k in range(NK):
                            m_mm = kb.em("tensor", lambda e, k=k, nh=nh, pmt=pmt, slot=slot: e.matmul(
                                pmt[:, :], slot[:, k * 128 : (k + 1) * 128],
                                z1[:, k * SC + nh * 512 : k * SC + (nh + 1) * 512],
                                start=(k == 0), stop=(k == NK - 1)),
                                waits=[(sa, m_z1), (sw, m_w)] + ([G[gk]] if k == 0 else []),
                                inc=(sp if k == NK - 1 else None))
                        kb.em("vector", lambda e, m=m, nh=nh: e.tensor_tensor(
                            scr1[:, nh * 512 : (nh + 1) * 512],
                            x0l[m][:, nh * 512 : (nh + 1) * 512],
                            rstd[:, nh * 512 : (nh + 1) * 512], ALU.mult),
                            waits=[(sv, m_n0), (sp, m_mm)])
                        kb.em("vector", lambda e, nh=nh: e.tensor_tensor(
                            scr1[:, nh * 512 : (nh + 1) * 512],
                            scr1[:, nh * 512 : (nh + 1) * 512],
                            negm[:, nh * 512 : (nh + 1) * 512], ALU.add))
                        m_ev = kb.em("vector", lambda e, m=m, nh=nh, pmt=pmt: e.scalar_tensor_tensor(
                            ao[m][:, nh * 512 : (nh + 1) * 512], pmt[:, :], opcol(m, 1),
                            scr1[:, nh * 512 : (nh + 1) * 512], ALU.add, ALU.add), inc=sv)
                        G[gk] = (sv, m_ev)
                        m_ao = m_ev
                        m_mm_last = m_mm
                    ring_free[sl] = (sp, m_mm_last)
                m_n1 = ln_stats2(ao, rstd, negm, (sv, m_ao))
                m_x1 = 0
                for k in range(NK):
                    kb.em("vector", lambda e, k=k: e.tensor_tensor(scr1[:, :], ao[k], rstd[:, :], ALU.mult),
                          waits=[(sv, m_n1)])
                    kb.em("vector", lambda e: e.tensor_tensor(scr1[:, :], scr1[:, :], negm[:, :], ALU.add))
                    m_x1 = kb.em("vector", lambda e, k=k: e.tensor_tensor(xt[k], x0l[k], scr1[:, :], ALU.add),
                                 inc=sv)
                if dbg is not None and c == 0:
                    for k in range(NK):
                        dmaS(dbg[n_layers + 1, k * 128 : (k + 1) * 128, :], xt[k], waits=[(sv, m_x1)])
                fhat = xiab[:, :].bitcast(F16)
                m_fh = 0
                for t in range(8):
                    for k in range(NK):
                        pmt = pm0 if k % 2 == 0 else pm1
                        gk = "pm0" if k % 2 == 0 else "pm1"
                        m_tr = kb.em("tensor", lambda e, k=k, t=t, pmt=pmt: e.transpose(
                            pmt[:, 0:128], xt[k][:, t * 128 : (t + 1) * 128], ident_sb[:, :]),
                            waits=[(sv, m_x1), G[gk]], inc=sp)
                        m_cp = kb.em("vector", lambda e, k=k, pmt=pmt: e.tensor_copy(
                            scr1[:, k * 128 : (k + 1) * 128], pmt[:, 0:128]),
                            waits=[(sp, m_tr)], inc=sv)
                        G[gk] = (sv, m_cp)
                    m_tt = sv.v
                    m_r0 = kb.em("vector", lambda e: e.tensor_reduce(lncol[:, 0:1], scr1[:, :], AXX, ALU.add),
                          waits=[(sv, m_tt)], inc=sv)
                    m_sq = kb.em("scalar", lambda e: e.activation(scr2[:, :], scr1[:, :], AF.Square),
                                 waits=[(sv, m_tt)], inc=sa)
                    m_r1c = kb.em("vector", lambda e: e.tensor_reduce(lncol[:, 1:2], scr2[:, :], AXX, ALU.add),
                          waits=[(sa, m_sq)], inc=sv)
                    m_cm = kb.em("gpsimd", lambda e: (
                        e.tensor_scalar(lncol[:, 2:3], lncol[:, 0:1], 1.0 / H, None, ALU.mult),
                        e.tensor_scalar(lncol[:, 3:4], lncol[:, 1:2], 1.0 / H, None, ALU.mult),
                        e.tensor_tensor(lncol[:, 4:5], lncol[:, 2:3], lncol[:, 2:3], ALU.mult),
                        e.tensor_tensor(lncol[:, 3:4], lncol[:, 3:4], lncol[:, 4:5], ALU.subtract))[-1],
                        waits=[(sv, m_r1c)], inc=sg, inck=1)
                    m_ln = kb.em("scalar", lambda e: e.activation(lncol[:, 5:6], lncol[:, 3:4], AF.Ln, bias=cst[:, 5:6]),
                                 waits=[(sg, m_cm)], inc=sa)
                    m_rs = kb.em("scalar", lambda e: e.activation(lncol[:, 6:7], lncol[:, 5:6], AF.Exp, scale=cst[:, 4:5]),
                                 inc=sa)
                    m_ng = kb.em("gpsimd", lambda e: (
                        e.tensor_tensor(lncol[:, 7:8], lncol[:, 2:3], lncol[:, 6:7], ALU.mult),
                        e.tensor_scalar(lncol[:, 7:8], lncol[:, 7:8], -1.0, None, ALU.mult))[-1],
                        waits=[(sa, m_rs)], inc=sg, inck=1)
                    m_ft = kb.em("scalar", lambda e: e.activation(
                        scr2[:, :], scr1[:, :], AF.Identity, bias=lncol[:, 7:8], scale=lncol[:, 6:7]),
                        waits=[(sg, m_ng)], inc=sa)
                    mq = quant_dequant(scr2[:, :], scr1[:, :], [(sa, m_ft)])
                    for k in range(NK):
                        pmt = pm0 if k % 2 == 0 else pm1
                        gk = "pm0" if k % 2 == 0 else "pm1"
                        m_tr = kb.em("tensor", lambda e, k=k, pmt=pmt: e.transpose(
                            pmt[:, 0:128], scr1[:, k * 128 : (k + 1) * 128], ident_sb[:, :]),
                            waits=[(sv, mq), G[gk]], inc=sp)
                        m_fh = kb.em("scalar", lambda e, k=k, t=t, pmt=pmt: e.copy(
                            fhat[:, k * SC + t * 128 : k * SC + (t + 1) * 128], pmt[:, 0:128]),
                            waits=[(sp, m_tr)], inc=sa)
                        G[gk] = (sa, m_fh)
                if dbg2 is not None and c == 0:
                    for k in range(NK):
                        kb.em("gpsimd", lambda e, k=k: e.dma_start(
                            out=dbg2[0, k * 128 : (k + 1) * 128, :], in_=fhat[:, k * SC : (k + 1) * SC]),
                            waits=[(sa, m_fh)], inc=sg, inck=16)
                ffo = [ar1[:, k * SC : (k + 1) * SC] for k in range(NK)]
                m_ffo = 0
                for m in range(8):
                    sl, slot, m_w = ring_load(OFF_FFN + m * BLK)
                    m_mm_last = 0
                    for nh in range(2):
                        pmt = pm0 if nh == 0 else pm1
                        gk = "pm0" if nh == 0 else "pm1"
                        m_mm = 0
                        for k in range(NK):
                            m_mm = kb.em("tensor", lambda e, k=k, nh=nh, pmt=pmt, slot=slot: e.matmul(
                                pmt[:, :], slot[:, k * 128 : (k + 1) * 128],
                                fhat[:, k * SC + nh * 512 : k * SC + (nh + 1) * 512],
                                start=(k == 0), stop=(k == NK - 1)),
                                waits=[(sa, m_fh), (sw, m_w), (sv, m_x1)] + ([G[gk]] if k == 0 else []),
                                inc=(sp if k == NK - 1 else None))
                        m_ev = kb.em("scalar", lambda e, m=m, nh=nh, pmt=pmt: e.activation(
                            ffo[m][:, nh * 512 : (nh + 1) * 512], pmt[:, :], AF.Identity,
                            bias=opcol(m, 2)),
                            waits=[(sp, m_mm)], inc=sa)
                        G[gk] = (sa, m_ev)
                        m_ffo = m_ev
                        m_mm_last = m_mm
                    ring_free[sl] = (sp, m_mm_last)
                if dbg2 is not None and c == 0:
                    for k in range(NK):
                        kb.em("gpsimd", lambda e, k=k: e.dma_start(
                            out=dbg2[1, k * 128 : (k + 1) * 128, :], in_=ffo[k]),
                            waits=[(sa, m_ffo)], inc=sg, inck=16)
                m_n2e = ln_stats2(ffo, rstd, negm, (sa, m_ffo))
                out_guard = [0, 0]
                for k in range(NK):
                    dd = scr2 if k % 2 == 0 else scr4
                    kb.em("vector", lambda e, k=k: e.tensor_tensor(scr1[:, :], ffo[k], rstd[:, :], ALU.mult),
                          waits=[(sv, m_n2e), (sout, out_guard[k % 2])])
                    kb.em("vector", lambda e: e.tensor_tensor(scr1[:, :], scr1[:, :], negm[:, :], ALU.add))
                    m_x2 = kb.em("vector", lambda e, k=k, dd=dd: e.tensor_tensor(dd[:, :], scr1[:, :], xt[k], ALU.add),
                                 inc=sv)
                    last_out = kb.em("sync", lambda e, k=k, c=c, dd=dd: e.dma_start(
                        out=out[k * 128 : (k + 1) * 128, c * SC : (c + 1) * SC], in_=dd[:, :]),
                        waits=[(sv, m_x2)], inc=sout, inck=16)
                    out_guard[k % 2] = last_out

        kb.em("sync", lambda e: e.nop(), waits=[(sout, sout.v)])
        kb.play(block)
    return nc


def host_prep(inputs, n_layers=NL):
    hs = np.asarray(inputs["L0_attn_hidden_states"], np.float32)
    in_proj = np.asarray(inputs["L0_attn_in_proj_w"], np.float32)
    conv_w = np.asarray(inputs["L0_attn_conv_w"], np.float32)
    conv_b = np.asarray(inputs["L0_attn_conv_b"], np.float32)
    x_proj = np.asarray(inputs["L0_attn_x_proj_w"], np.float32)
    dt_w = np.asarray(inputs["L0_attn_dt_proj_w"], np.float32)
    dt_b = np.asarray(inputs["L0_attn_dt_proj_b"], np.float32)
    Dp = np.asarray(inputs["L0_attn_D"], np.float32)
    mg = np.asarray(inputs["L0_attn_mamba_ln_g"], np.float32)
    mb = np.asarray(inputs["L0_attn_mamba_ln_b"], np.float32)
    outp = np.asarray(inputs["L0_attn_out_proj_w"], np.float32)
    lng = np.asarray(inputs["L0_attn_ln_g"], np.float32)
    lnb = np.asarray(inputs["L0_attn_ln_b"], np.float32)
    ing = np.asarray(inputs["L0_attn_in_norm_g"], np.float32)
    inb = np.asarray(inputs["L0_attn_in_norm_b"], np.float32)
    op1 = np.asarray(inputs["L0_attn_op1_w"], np.float32)
    op1b = np.asarray(inputs["L0_attn_op1_b"], np.float32)
    op2 = np.asarray(inputs["L0_attn_op2_w"], np.float32)
    op2b = np.asarray(inputs["L0_attn_op2_b"], np.float32)
    ffw = np.asarray(inputs["L0_ffn_weight"], np.float32)
    ffb = np.asarray(inputs["L0_ffn_bias"], np.float32)

    ident = np.eye(128, dtype=np.float32)
    sel = np.zeros((33, 33 * 128), np.float16)
    for n in range(33):
        sel[n, n * 128 : (n + 1) * 128] = 1.0

    phh_a = np.zeros((H, 12), np.float32)
    phh_a[:, 0:4] = lng.T
    phh_a[:, 4:8] = lnb.T
    phh_a[:, 8] = ing
    phh_a[:, 9] = inb
    pop_a = np.zeros((H, 4), np.float32)
    pop_a[:, 0] = op1b
    pop_a[:, 1] = op2b
    pop_a[:, 2] = ffb

    in_maps = []
    for core in range(8):
        b, hf = core // 2, core % 2
        cols = slice(hf * 1024, (hf + 1) * 1024)
        winT_c = np.zeros((NL, H, 2048), np.float32)
        woutT_c = np.zeros((NL, H, H), np.float32)
        wxT_c = np.zeros((NL, H, 33), np.float32)
        psml_c = np.zeros((NL * H, 16), np.float32)
        for l in range(NL):
            wi = np.concatenate([in_proj[l][hf * 1024 : (hf + 1) * 1024],
                                 in_proj[l][2048 + hf * 1024 : 2048 + (hf + 1) * 1024]], axis=0)
            winT_c[l] = np.ascontiguousarray(wi.T)
            woutT_c[l] = np.ascontiguousarray(outp[l][:, cols].T)
            wxT_c[l] = np.ascontiguousarray(x_proj[l][:, cols].T)
            pm = np.zeros((H, 16), np.float32)
            pm[:, 0:4] = conv_w[l, 0][cols, :]
            pm[:, 4:8] = conv_w[l, 1][cols, :]
            pm[:, 8] = conv_b[l, 0][cols]
            pm[:, 9] = conv_b[l, 1][cols]
            pm[:, 10] = dt_w[l, cols, 0]
            pm[:, 11] = dt_b[l, cols]
            pm[:, 12] = Dp[l, cols]
            pm[:, 13] = mg[l, cols]
            pm[:, 14] = mb[l, cols]
            psml_c[l * H : (l + 1) * H] = pm
        in_maps.append({
            "x0t": np.ascontiguousarray(hs[b].T),
            "winT": winT_c,
            "woutT": woutT_c,
            "wxT": wxT_c,
            "psml": psml_c,
            "phh": phh_a,
            "pop": pop_a,
            "wop1T": np.ascontiguousarray(op1.T),
            "wop2T": np.ascontiguousarray(op2.T),
            "wffn": ffw,
            "identf": ident,
            "self16": sel,
        })
    return in_maps


_NC_CACHE = {}


def kernel(**inputs):
    key = "full"
    if key not in _NC_CACHE:
        _NC_CACHE[key] = build_program()
    nc = _NC_CACHE[key]
    in_maps = host_prep(inputs)
    res = run_bass_kernel_spmd(nc, in_maps, core_ids=list(range(8)))
    outs = []
    for b in range(4):
        o = np.asarray(res.results[2 * b]["out"], np.float32)
        outs.append(o.T)
    return np.stack(outs, axis=0)
